# version 21
# baseline (speedup 1.0000x reference)
import numpy as np
import ml_dtypes
from contextlib import ExitStack

import concourse.bass as bass
import concourse.mybir as mybir
from concourse.bass_utils import run_bass_kernel_spmd

F32 = mybir.dt.float32
BF16 = mybir.dt.bfloat16
AF = mybir.ActivationFunctionType
ALU = mybir.AluOpType
AX = mybir.AxisListType

EPS = 1e-5
R = 16
LORA_SCALING = 0.5


class Sem:

    def __init__(self, nc, name):
        self.h = nc.alloc_semaphore(name)
        self.v = 0

    def inc(self, instr, k=1):
        instr.then_inc(self.h, k)
        self.v += k
        return self.v

    def dinc(self, instr):
        return self.inc(instr, 16)


def build_nc(T, D):
    HD = 128
    H = D // HD
    NCH = D // 128
    NT = T // 128
    NQC = T // 512
    NKG = NT // 2
    SCALE = float(HD) ** -0.5
    INV_D = 1.0 / D
    NEG_INV_D = -1.0 / D
    NHF = D // 1024

    assert T % 512 == 0 and NT % 2 == 0 and D % 1024 == 0

    nc = bass.Bass("TRN2", target_bir_lowering=False)

    x_d = nc.dram_tensor("x", [T, D], F32, kind="ExternalInput")
    wq_d = nc.dram_tensor("wq", [D, D], F32, kind="ExternalInput")
    wk_d = nc.dram_tensor("wk", [D, D], F32, kind="ExternalInput")
    wv_d = nc.dram_tensor("wv", [D, D], F32, kind="ExternalInput")
    wo_d = nc.dram_tensor("wo", [D, D], F32, kind="ExternalInput")
    bq_d = nc.dram_tensor("bq", [D], F32, kind="ExternalInput")
    bk_d = nc.dram_tensor("bk", [D], F32, kind="ExternalInput")
    bv_d = nc.dram_tensor("bv", [D], F32, kind="ExternalInput")
    bo_d = nc.dram_tensor("bo", [1, D], F32, kind="ExternalInput")
    Bq_d = nc.dram_tensor("Blq", [R, D], F32, kind="ExternalInput")
    Bk_d = nc.dram_tensor("Blk", [R, D], F32, kind="ExternalInput")
    Bv_d = nc.dram_tensor("Blv", [R, D], F32, kind="ExternalInput")
    Bo_d = nc.dram_tensor("Blo", [R, D], F32, kind="ExternalInput")
    Aq_d = nc.dram_tensor("Alq", [D, R], F32, kind="ExternalInput")
    Ak_d = nc.dram_tensor("Alk", [D, R], F32, kind="ExternalInput")
    Av_d = nc.dram_tensor("Alv", [D, R], F32, kind="ExternalInput")
    Ao_d = nc.dram_tensor("Alo", [D, R], F32, kind="ExternalInput")
    gm_d = nc.dram_tensor("gm", [1, D], F32, kind="ExternalInput")
    bt_d = nc.dram_tensor("bt", [1, D], F32, kind="ExternalInput")
    fw_d = nc.dram_tensor("fw", [D, D], F32, kind="ExternalInput")
    fb_d = nc.dram_tensor("fb", [1, D], F32, kind="ExternalInput")
    al_d = nc.dram_tensor("al", [1, D], F32, kind="ExternalInput")
    out_d = nc.dram_tensor("out", [T, D], F32, kind="ExternalOutput")

    s_qT = nc.dram_tensor("s_qT", [D, T], BF16, kind="Internal")
    s_kT = nc.dram_tensor("s_kT", [D, T], BF16, kind="Internal")
    s_vT = nc.dram_tensor("s_vT", [D, T], BF16, kind="Internal")
    s_a = nc.dram_tensor("s_a", [D, T], BF16, kind="Internal")
    s_o = nc.dram_tensor("s_o", [D, T], BF16, kind="Internal")

    eye_d = nc.inline_tensor(np.eye(128, dtype=ml_dtypes.bfloat16), name="eye128")

    CONST_VALS = [EPS, -0.5, LORA_SCALING, INV_D, NEG_INV_D, SCALE, -1.0, 1.002]
    cst_tile = nc.alloc_sbuf_tensor("cstcols", [128, len(CONST_VALS)], F32)
    for i, v in enumerate(CONST_VALS):
        nc.const_aps.register(F32, float(v), cst_tile.ap()[:, i:i + 1])

    ident = nc.alloc_sbuf_tensor("ident", [128, 128], BF16)
    ones_f32r = nc.alloc_sbuf_tensor("ones_f32r", [1, 128], F32)
    ones_bfr = nc.alloc_sbuf_tensor("ones_bfr", [1, 128], BF16)
    ones_bfc = nc.alloc_sbuf_tensor("ones_bfc", [128, 1], BF16)
    ATqkv = nc.alloc_sbuf_tensor("ATqkv", [128, D], BF16)
    ATo = nc.alloc_sbuf_tensor("ATo", [16, D], BF16)
    BsT = nc.alloc_sbuf_tensor("BsT", [128, NCH * 80], BF16)
    BoT = nc.alloc_sbuf_tensor("BoT", [128, NCH * 16], BF16)
    bqs_c = nc.alloc_sbuf_tensor("bqs_c", [128, NCH], F32)
    bk_c = nc.alloc_sbuf_tensor("bk_c", [128, NCH], F32)
    bv_c = nc.alloc_sbuf_tensor("bv_c", [128, NCH], F32)

    sems = {}

    def S(name):
        if name not in sems:
            sems[name] = Sem(nc, name)
        return sems[name]

    band = {0: 0, 1: 32, 2: 64}
    PHASES = []

    es_xn = ExitStack()
    xn_fm = es_xn.enter_context(nc.sbuf_tensor("xn_fm", [128, NCH * T], BF16))

    es_p1 = ExitStack()
    gb_g = es_p1.enter_context(nc.sbuf_tensor("gb_g", [128, D], F32))
    gb_b = es_p1.enter_context(nc.sbuf_tensor("gb_b", [128, D], F32))
    xf = [es_p1.enter_context(nc.sbuf_tensor(f"xf{i}", [128, D], F32)) for i in range(2)]
    xn32 = es_p1.enter_context(nc.sbuf_tensor("xn32", [128, D], F32))
    xnbf = es_p1.enter_context(nc.sbuf_tensor("xnbf", [128, D], BF16))
    abf = es_p1.enter_context(nc.sbuf_tensor("abf", [128, D], BF16))
    a_acc = es_p1.enter_context(nc.sbuf_tensor("a_acc", [128, NCH * 256], BF16))
    stat = es_p1.enter_context(nc.sbuf_tensor("stat", [128, 8], F32))
    ptrA = es_p1.enter_context(nc.psum_tensor("ptrA", [128, 1024], BF16))
    ptrB = es_p1.enter_context(nc.psum_tensor("ptrB", [128, 1024], BF16))

    es_b0 = ExitStack()
    braw1 = es_b0.enter_context(nc.sbuf_tensor("braw1", [16, D // 2], F32))
    brawb1 = es_b0.enter_context(nc.sbuf_tensor("brawb1", [16, D // 2], BF16))
    araw1 = es_b0.enter_context(nc.sbuf_tensor("araw1", [128, (NCH // 2) * 16], F32))
    arawb1 = es_b0.enter_context(nc.sbuf_tensor("arawb1", [128, (NCH // 2) * 16], BF16))
    p_t16 = es_b0.enter_context(nc.psum_tensor("p_t16", [128, 1024], BF16))

    def _phase0(sync):
        ld = S("b0_ld")
        gp = S("b0_gp")
        pe = S("b0_pe")
        dv = S("b0_dv")
        ac = S("b0_ac")

        for i, v in enumerate(CONST_VALS):
            nc.gpsimd.memset(cst_tile.ap()[:, i:i + 1], float(v))
        nc.gpsimd.memset(ones_f32r.ap(), 1.0)
        nc.gpsimd.memset(ones_bfr.ap(), 1.0)
        nc.gpsimd.memset(ones_bfc.ap(), 1.0)
        nc.gpsimd.memset(BsT.ap(), 0.0)
        gp.inc(nc.gpsimd.memset(ATqkv.ap(), 0.0))

        ld.dinc(sync.dma_start(ident.ap(), eye_d.ap()))
        for c in range(NCH):
            ld.dinc(sync.dma_start(bqs_c.ap()[:, c:c + 1], bq_d.ap()[c * 128:(c + 1) * 128]))
            ld.dinc(sync.dma_start(bk_c.ap()[:, c:c + 1], bk_d.ap()[c * 128:(c + 1) * 128]))
            ld.dinc(sync.dma_start(bv_c.ap()[:, c:c + 1], bv_d.ap()[c * 128:(c + 1) * 128]))
        nc.scalar.wait_ge(ld.h, ld.v)
        nc.scalar.wait_ge(gp.h, gp.v)
        ac.inc(nc.scalar.mul(bqs_c.ap(), bqs_c.ap(), SCALE))

        ld2 = S("b0_ld2")
        nc.tensor.wait_ge(ld.h, ld.v)
        HCH = NCH // 2
        for i, bd in enumerate([Bq_d, Bk_d, Bv_d, Bo_d]):
            for hf in range(2):
                if dv.v:
                    sync.wait_ge(dv.h, dv.v)
                lv = ld2.dinc(sync.dma_start(
                    braw1.ap(), bd.ap()[:, hf * (D // 2):(hf + 1) * (D // 2)]))
                nc.vector.wait_ge(ld2.h, lv)
                if ld2.v > 16:
                    nc.vector.wait_ge(pe.h, pe.v)
                cv = dv.inc(nc.vector.tensor_copy(brawb1.ap(), braw1.ap()))
                nc.tensor.wait_ge(dv.h, cv)
                nc.tensor.wait_ge(gp.h, gp.v)
                for cc in range(HCH):
                    c = hf * HCH + cc
                    ins = nc.tensor.transpose(p_t16[:, cc * 16:(cc + 1) * 16],
                                              brawb1[:, cc * 128:(cc + 1) * 128],
                                              ident[0:16, 0:16])
                    if cc == HCH - 1:
                        pe.inc(ins)
                nc.vector.wait_ge(pe.h, pe.v)
                for cc in range(HCH):
                    c = hf * HCH + cc
                    if i < 3:
                        dst = BsT[:, c * 80 + band[i]: c * 80 + band[i] + 16]
                    else:
                        dst = BoT[:, c * 16:(c + 1) * 16]
                    ins = nc.vector.tensor_copy(dst, p_t16[:, cc * 16:(cc + 1) * 16])
                    if cc == HCH - 1:
                        dv.inc(ins)
                nc.tensor.wait_ge(dv.h, dv.v)

        for i, ad in enumerate([Aq_d, Ak_d, Av_d, Ao_d]):
            for hf in range(2):
                c0 = hf * HCH
                sync.wait_ge(dv.h, dv.v)
                lv = ld2.dinc(sync.dma_start(
                    araw1.ap().rearrange("p (c r) -> p c r", r=R),
                    ad.ap().rearrange("(c p) r -> p c r", p=128)[:, c0:c0 + HCH, :]))
                nc.vector.wait_ge(ld2.h, lv)
                nc.vector.wait_ge(pe.h, pe.v)
                cv = dv.inc(nc.vector.tensor_copy(arawb1.ap(), araw1.ap()))
                nc.tensor.wait_ge(dv.h, cv)
                for cc in range(HCH):
                    ins = nc.tensor.transpose(p_t16[0:16, cc * 128:(cc + 1) * 128],
                                              arawb1[:, cc * 16:(cc + 1) * 16],
                                              ident[:])
                    if cc == HCH - 1:
                        pe.inc(ins)
                nc.vector.wait_ge(pe.h, pe.v)
                for cc in range(HCH):
                    c = c0 + cc
                    if i < 3:
                        dst = ATqkv[band[i]:band[i] + 16, c * 128:(c + 1) * 128]
                    else:
                        dst = ATo[0:16, c * 128:(c + 1) * 128]
                    ins = nc.vector.tensor_copy(dst, p_t16[0:16, cc * 128:(cc + 1) * 128])
                    if cc == HCH - 1:
                        dv.inc(ins)
                nc.tensor.wait_ge(dv.h, dv.v)

    PHASES.append(_phase0)
    es_b0.close()


    xn_v = xn_fm.ap().rearrange("p (c n) -> p c n", n=T)
    aac_v = a_acc.ap().rearrange("p (c n) -> p c n", n=256)

    def _phase1(sync):
        ldx = [S("p1_ldx0"), S("p1_ldx1")]
        dv = S("p1_dv")
        ac = S("p1_ac")
        pe = S("p1_pe")
        sta = S("p1_sta")
        evA = S("p1_evA")
        evB = S("p1_evB")
        xf_last = {}
        abf_free = [0]
        xnbf_free = [0]

        def stats_pass(src, scratch, act_dv_wait=None, act_pe_wait=None,
                       dv_wait=None):
            if dv_wait is not None:
                nc.vector.wait_ge(dv.h, dv_wait)
            v_sum = dv.inc(nc.vector.reduce_sum(stat[:, 0:1], src, axis=AX.X))
            nc.vector.wait_ge(dv.h, v_sum)
            v_mu = dv.inc(nc.vector.tensor_scalar_mul(stat[:, 2:3], stat[:, 0:1], INV_D))
            nc.vector.wait_ge(dv.h, v_mu)
            v_nm = dv.inc(nc.vector.tensor_scalar_mul(stat[:, 3:4], stat[:, 2:3], -1.0))
            if act_pe_wait is not None:
                nc.scalar.wait_ge(pe.h, act_pe_wait)
            if act_dv_wait is not None:
                nc.scalar.wait_ge(dv.h, act_dv_wait)
            nc.scalar.wait_ge(dv.h, v_nm)
            a_sq = ac.inc(nc.scalar.activation(scratch, src, AF.Square,
                                               bias=stat[:, 3:4],
                                               accum_out=stat[:, 1:2]))
            nc.vector.wait_ge(ac.h, a_sq)
            v_var = dv.inc(nc.vector.tensor_scalar_mul(stat[:, 4:5], stat[:, 1:2], INV_D))
            nc.scalar.wait_ge(dv.h, v_var)
            a_ln = ac.inc(nc.scalar.activation(stat[:, 7:8], stat[:, 4:5], AF.Ln,
                                               bias=EPS))
            nc.scalar.wait_ge(ac.h, a_ln)
            a_r = ac.inc(nc.scalar.activation(stat[:, 5:6], stat[:, 7:8], AF.Exp,
                                              scale=-0.5))
            nc.vector.wait_ge(ac.h, a_r)
            v_nmr = dv.inc(nc.vector.tensor_scalar(
                stat[:, 6:7], stat[:, 2:3], stat[:, 5:6], -1.0, ALU.mult, ALU.mult))
            return v_nmr, a_sq

        gld = S("p1_gld")
        gld.dinc(sync.dma_start(xf[0].ap()[0:1, :], gm_d.ap()))
        gld.dinc(sync.dma_start(xf[1].ap()[0:1, :], bt_d.ap()))
        nc.tensor.wait_ge(gld.h, gld.v)
        for src_row, dst in ((xf[0], gb_g), (xf[1], gb_b)):
            for j in range(D // 512):
                if dv.v:
                    nc.tensor.wait_ge(dv.h, dv.v)
                pe.inc(nc.tensor.matmul(
                    ptrA[:, 0:1024].bitcast(F32), ones_f32r.ap(),
                    src_row.ap()[0:1, j * 512:(j + 1) * 512], start=True, stop=True))
                nc.vector.wait_ge(pe.h, pe.v)
                dv.inc(nc.vector.tensor_copy(dst[:, j * 512:(j + 1) * 512],
                                             ptrA[:, 0:1024].bitcast(F32)))
        nc.vector.wait_ge(dv.h, dv.v)

        sync.wait_ge(pe.h, pe.v)
        ldx[0].dinc(sync.dma_start(xf[0].ap(), x_d.ap()[0:128, :]))

        for t in range(NT):
            sl = t % 2
            if t + 1 < NT:
                if t >= 1:
                    pdv, pac = xf_last[t - 1]
                    sync.wait_ge(dv.h, pdv)
                    sync.wait_ge(ac.h, pac)
                ldx[(t + 1) % 2].dinc(sync.dma_start(
                    xf[(t + 1) % 2].ap(), x_d.ap()[(t + 1) * 128:(t + 2) * 128, :]))

            nc.vector.wait_ge(ldx[sl].h, ldx[sl].v)
            nc.scalar.wait_ge(ldx[sl].h, ldx[sl].v)
            v_nmr, a_sq1 = stats_pass(xf[sl].ap(), abf.ap(),
                                      act_pe_wait=abf_free[0])
            nc.vector.wait_ge(dv.h, v_nmr)
            v_xn = dv.inc(nc.vector.tensor_scalar(
                xn32.ap(), xf[sl].ap(), stat[:, 5:6], stat[:, 6:7],
                ALU.mult, ALU.add))
            xf_last[t] = (v_xn, a_sq1)
            v_nmr2, a_sq2 = stats_pass(xn32.ap(), xnbf.ap(), act_dv_wait=v_xn,
                                       act_pe_wait=xnbf_free[0], dv_wait=v_xn)
            nc.vector.wait_ge(dv.h, v_xn)
            nc.vector.wait_ge(ac.h, a_sq2)
            v_xnbf = dv.inc(nc.vector.tensor_copy(xnbf.ap(), xn32.ap()))
            nc.vector.wait_ge(dv.h, v_nmr2)
            v_u = dv.inc(nc.vector.tensor_scalar(
                xn32.ap(), xn32.ap(), stat[:, 5:6], stat[:, 6:7],
                ALU.mult, ALU.add))
            nc.vector.wait_ge(dv.h, v_u)
            v_g = dv.inc(nc.vector.tensor_tensor(xn32.ap(), xn32.ap(), gb_g.ap(), ALU.mult))
            nc.vector.wait_ge(dv.h, v_g)
            v_b = dv.inc(nc.vector.tensor_tensor(xn32.ap(), xn32.ap(), gb_b.ap(), ALU.add))
            nc.scalar.wait_ge(dv.h, v_b)
            a_s = ac.inc(nc.scalar.activation(abf.ap(), xn32.ap(), AF.Sign))

            nhalf = min(8, NCH)
            nc.tensor.wait_ge(dv.h, v_xnbf)
            nc.tensor.wait_ge(evA.h, evA.v)
            for j in range(nhalf):
                ins = nc.tensor.transpose(ptrA[:, j * 128:(j + 1) * 128],
                                          xnbf[:, j * 128:(j + 1) * 128], ident[:])
                if j == nhalf - 1:
                    pe.inc(ins)
            nc.vector.wait_ge(pe.h, pe.v)
            evA.inc(nc.vector.tensor_copy(
                xn_v[:, 0:nhalf, t * 128:(t + 1) * 128],
                ptrA[:].rearrange("p (c n) -> p c n", n=128)[:, 0:nhalf, :]))
            if NCH > 8:
                nc.tensor.wait_ge(evB.h, evB.v)
                for j in range(8, NCH):
                    ins = nc.tensor.transpose(ptrB[:, (j - 8) * 128:(j - 7) * 128],
                                              xnbf[:, j * 128:(j + 1) * 128], ident[:])
                    if j == NCH - 1:
                        pe.inc(ins)
                nc.vector.wait_ge(pe.h, pe.v)
                evB.inc(nc.vector.tensor_copy(
                    xn_v[:, 8:NCH, t * 128:(t + 1) * 128],
                    ptrB[:].rearrange("p (c n) -> p c n", n=128)[:, 0:NCH - 8, :]))
            xnbf_free[0] = pe.v

            nc.tensor.wait_ge(ac.h, a_s)
            nc.tensor.wait_ge(evA.h, evA.v)
            for j in range(nhalf):
                ins = nc.tensor.transpose(ptrA[:, j * 128:(j + 1) * 128],
                                          abf[:, j * 128:(j + 1) * 128], ident[:])
                if j == nhalf - 1:
                    pe.inc(ins)
            nc.vector.wait_ge(pe.h, pe.v)
            if t % 2 == 0 and t >= 2:
                nc.vector.wait_ge(sta.h, sta.v)
            evA.inc(nc.vector.tensor_copy(
                aac_v[:, 0:nhalf, (t % 2) * 128:(t % 2) * 128 + 128],
                ptrA[:].rearrange("p (c n) -> p c n", n=128)[:, 0:nhalf, :]))
            if NCH > 8:
                nc.tensor.wait_ge(evB.h, evB.v)
                for j in range(8, NCH):
                    ins = nc.tensor.transpose(ptrB[:, (j - 8) * 128:(j - 7) * 128],
                                              abf[:, j * 128:(j + 1) * 128], ident[:])
                    if j == NCH - 1:
                        pe.inc(ins)
                nc.vector.wait_ge(pe.h, pe.v)
                evB.inc(nc.vector.tensor_copy(
                    aac_v[:, 8:NCH, (t % 2) * 128:(t % 2) * 128 + 128],
                    ptrB[:].rearrange("p (c n) -> p c n", n=128)[:, 0:NCH - 8, :]))

            abf_free[0] = pe.v

            if t % 2 == 1:
                nc.scalar.wait_ge(evA.h, evA.v)
                if NCH > 8:
                    nc.scalar.wait_ge(evB.h, evB.v)
                sta.dinc(nc.scalar.dma_start(
                    s_a.ap().rearrange("(c p) n -> p c n", p=128)[:, :,
                                                                 (t - 1) * 128:(t + 1) * 128],
                    aac_v))

        sync.wait_ge(ldx[0].h, ldx[0].v)
        sync.wait_ge(ldx[1].h, ldx[1].v)
        nc.scalar.wait_ge(sta.h, sta.v)

    PHASES.append(_phase1)
    es_p1.close()

    es_p2 = ExitStack()
    wbuf = [es_p2.enter_context(nc.sbuf_tensor(f"wbuf{i}", [128, D], F32)) for i in range(2)]
    wbf = [es_p2.enter_context(nc.sbuf_tensor(f"wbf{i}", [128, D], BF16)) for i in range(2)]
    wT = [es_p2.enter_context(nc.sbuf_tensor(f"wT{i}", [128, D], BF16)) for i in range(2)]
    stg = [es_p2.enter_context(nc.sbuf_tensor(f"stg{i}", [128, 1536], BF16)) for i in range(2)]
    xBs = es_p2.enter_context(nc.sbuf_tensor("xBs", [128, T], BF16))
    pgA = es_p2.enter_context(nc.psum_tensor("pgA", [128, 1536], F32))
    pgB = es_p2.enter_context(nc.psum_tensor("pgB", [128, 1536], F32))
    ptrW = es_p2.enter_context(nc.psum_tensor("ptrW", [128, 1024], BF16))

    chunk_groups = []
    s0 = 0
    while s0 < NQC:
        n = min(3, NQC - s0)
        chunk_groups.append((s0, n))
        s0 += n
    NG = len(chunk_groups)

    def _phase2(sync):
        ldw = [S("p2_ldw0"), S("p2_ldw1")]
        pe = S("p2_pe")
        dv = S("p2_dv")
        ac = S("p2_ac")
        stq2 = [S("p2_stq0"), S("p2_stq1")]
        cast_v = {0: 0, 1: 0}
        wfree_v = {0: 0, 1: 0}
        pgev_v = {0: None, 1: None}
        stgf_v = {0: 0, 1: 0}

        for tc in range(NQC):
            if tc >= 1:
                nc.tensor.wait_ge(ac.h, ac.v)
            for ic in range(NCH):
                pe.inc(nc.tensor.matmul(
                    pgA[0:80, 0:512],
                    BsT[:, ic * 80:(ic + 1) * 80],
                    xn_fm[:, ic * T + tc * 512: ic * T + (tc + 1) * 512],
                    start=(ic == 0), stop=(ic == NCH - 1)))
            nc.scalar.wait_ge(pe.h, pe.v)
            ac.inc(nc.scalar.activation(xBs[0:80, tc * 512:(tc + 1) * 512],
                                        pgA[0:80, 0:512], AF.Copy,
                                        scale=LORA_SCALING))
        pgev_v[0] = (ac, ac.v)

        projs = [
            ("q", wq_d, s_qT, True, bqs_c, 0),
            ("k", wk_d, s_kT, False, bk_c, 32),
            ("v", wv_d, s_vT, False, bv_c, 64),
        ]
        ldw[0].dinc(sync.dma_start(wbuf[0].ap(), wq_d.ap()[0:128, :]))
        it = 0
        gidx = 0
        for pi, (pname, w_d, dst_d, scaled, bias_c, bandofs) in enumerate(projs):
            for och in range(NCH):
                sl = it % 2
                nxt = None
                if och + 1 < NCH:
                    nxt = (w_d, och + 1)
                elif pi + 1 < len(projs):
                    nxt = (projs[pi + 1][1], 0)
                if nxt is not None:
                    if it >= 1:
                        sync.wait_ge(dv.h, cast_v[(it + 1) % 2])
                    ldw[(it + 1) % 2].dinc(sync.dma_start(
                        wbuf[(it + 1) % 2].ap(),
                        nxt[0].ap()[nxt[1] * 128:(nxt[1] + 1) * 128, :]))

                nc.vector.wait_ge(ldw[sl].h, ldw[sl].v)
                if it >= 2:
                    nc.vector.wait_ge(pe.h, wfree_v[sl])
                cv = dv.inc(nc.vector.tensor_copy(wbf[sl].ap(), wbuf[sl].ap()))
                cast_v[sl] = cv

                nc.tensor.wait_ge(dv.h, cv)
                for half in range(0, NCH, 8):
                    nh = min(8, NCH - half)
                    if half > 0 or it > 0:
                        nc.tensor.wait_ge(dv.h, dv.v)
                    for j in range(nh):
                        ins = nc.tensor.transpose(
                            ptrW[:, j * 128:(j + 1) * 128],
                            wbf[sl][:, (half + j) * 128:(half + j + 1) * 128],
                            ident[:])
                        if j == nh - 1:
                            pe.inc(ins)
                    nc.vector.wait_ge(pe.h, pe.v)
                    dv.inc(nc.vector.tensor_copy(
                        wT[sl][:, half * 128:(half + nh) * 128],
                        ptrW[:, 0:nh * 128]))
                wT_ready = dv.v

                nc.tensor.wait_ge(dv.h, wT_ready)
                for (c0, ncks) in chunk_groups:
                    psl = gidx % 2
                    pg = pgA if psl == 0 else pgB
                    if pgev_v[psl] is not None:
                        psem, pval = pgev_v[psl]
                        nc.tensor.wait_ge(psem.h, pval)
                    for ic in range(NCH):
                        for k in range(ncks):
                            tc = c0 + k
                            pe.inc(nc.tensor.matmul(
                                pg[:, k * 512:(k + 1) * 512],
                                wT[sl][:, ic * 128:(ic + 1) * 128],
                                xn_fm[:, ic * T + tc * 512: ic * T + (tc + 1) * 512],
                                start=(ic == 0), stop=False,
                                skip_group_check=True))
                    for k in range(ncks):
                        tc = c0 + k
                        pe.inc(nc.tensor.matmul(
                            pg[:, k * 512:(k + 1) * 512],
                            ATqkv[bandofs:bandofs + 16, och * 128:(och + 1) * 128],
                            xBs[bandofs:bandofs + 16, tc * 512:(tc + 1) * 512],
                            start=False, stop=True, skip_group_check=True))
                    wfree_v[sl] = pe.v
                    nc.vector.wait_ge(pe.h, pe.v)
                    if stgf_v[psl]:
                        nc.vector.wait_ge(stq2[psl].h, stgf_v[psl])
                    v_ev = dv.inc(nc.vector.tensor_scalar(
                        stg[psl][:, 0:ncks * 512], pg[:, 0:ncks * 512],
                        (SCALE if scaled else 1.0),
                        bias_c.ap()[:, och:och + 1], ALU.mult, ALU.add))
                    pgev_v[psl] = (dv, v_ev)
                    nc.scalar.wait_ge(dv.h, v_ev)
                    stq2[psl].dinc(nc.scalar.dma_start(
                        dst_d.ap()[och * 128:(och + 1) * 128,
                                   c0 * 512:(c0 + ncks) * 512],
                        stg[psl][:, 0:ncks * 512]))
                    stgf_v[psl] = stq2[psl].v
                    gidx += 1
                it += 1

        sync.wait_ge(ldw[0].h, ldw[0].v)
        sync.wait_ge(ldw[1].h, ldw[1].v)
        nc.scalar.wait_ge(stq2[0].h, stq2[0].v)
        nc.scalar.wait_ge(stq2[1].h, stq2[1].v)

    PHASES.append(_phase2)
    es_p2.close()
    es_xn.close()

    es_p3 = ExitStack()
    kT_b = [es_p3.enter_context(nc.sbuf_tensor(f"kT{i}", [128, T], BF16)) for i in range(2)]
    qT_b = [es_p3.enter_context(nc.sbuf_tensor(f"qT{i}", [128, T], BF16)) for i in range(2)]
    vTf_b = [es_p3.enter_context(nc.sbuf_tensor(f"vTf{i}", [128, T], BF16)) for i in range(2)]
    vh2 = [es_p3.enter_context(nc.sbuf_tensor(f"vh{i}", [128, T], BF16)) for i in range(2)]
    pT2 = [es_p3.enter_context(nc.sbuf_tensor(f"pT{i}", [128, NT * 512], BF16)) for i in range(2)]
    sq3b = [es_p3.enter_context(nc.sbuf_tensor(f"sq3{i}", [128, T], BF16)) for i in range(2)]
    mxp = es_p3.enter_context(nc.sbuf_tensor("mxp", [1, NQC], F32))
    mxs = es_p3.enter_context(nc.sbuf_tensor("mxs", [1, 4], F32))
    negc2 = [es_p3.enter_context(nc.sbuf_tensor(f"negc{i}", [128, 1], F32)) for i in range(2)]
    rec2 = [es_p3.enter_context(nc.sbuf_tensor(f"rec2{i}", [1, 512], F32)) for i in range(2)]
    obuf = [es_p3.enter_context(nc.sbuf_tensor(f"obuf{i}", [128, 512], F32)) for i in range(2)]
    ptree2 = [es_p3.enter_context(nc.sbuf_tensor(f"ptree{i}", [128, NT * 256], BF16)) for i in range(2)]
    ostg = [es_p3.enter_context(nc.sbuf_tensor(f"ostg{i}", [128, 512], BF16)) for i in range(2)]
    sg3 = [es_p3.enter_context(nc.psum_tensor(f"sg3{i}", [128, 1024], F32)) for i in range(3)]
    avp = [es_p3.enter_context(nc.psum_tensor(f"avp{i}", [128, 512], F32)) for i in range(2)]

    def _phase3(sync):
        ldk = [S("p3_ldk0"), S("p3_ldk1")]
        ldq = [S("p3_ldq0"), S("p3_ldq1")]
        ldv = [S("p3_ldv0"), S("p3_ldv1")]
        pe = S("p3_pe")
        dv = S("p3_dv")
        ac = S("p3_ac")
        sto3 = [S("p3_sto0"), S("p3_sto1")]
        av_done = 0
        av_done_slot = {0: 0, 1: 0}
        ost_done = {0: 0, 1: 0}
        avfree = {0: 0, 1: 0}
        rot = {"i": 0}
        bank_waits = {0: [], 1: [], 2: []}
        sq3_free = {0: 0, 1: 0}
        vh_slot_free = {0: 0, 1: 0}
        tree_a = [0]
        negc_ready = {0: 0, 1: 0}


        def take_bank():
            b = rot["i"] % 3
            rot["i"] += 1
            for (sem, val) in bank_waits[b]:
                nc.tensor.wait_ge(sem.h, val)
            bank_waits[b] = []
            return b

        pending_sums = None
        pending_fin = None

        def emit_fin_stage(pend, cur_qsl):
            ph, pq0, pqsl, pv_ob = pend
            nc.tensor.wait_ge(dv.h, pv_ob)
            nc.tensor.wait_ge(dv.h, rec_ready[pqsl])
            pv = pe.inc(nc.tensor.matmul(avp[cur_qsl][:], ones_f32r.ap(),
                                         rec2[pqsl][0:1, :], start=True, stop=True))
            rec_read[pqsl] = pv
            nc.vector.wait_ge(pe.h, pv)
            nc.vector.wait_ge(dv.h, pv_ob)
            if ost_done[pqsl]:
                nc.vector.wait_ge(sto3[pqsl].h, ost_done[pqsl])
            v_os = dv.inc(nc.vector.tensor_tensor(ostg[pqsl].ap(), obuf[pqsl].ap(),
                                                  avp[cur_qsl][:], ALU.mult))
            avfree[cur_qsl] = v_os
            nc.scalar.wait_ge(dv.h, v_os)
            sto3[pqsl].dinc(nc.scalar.dma_start(
                s_o.ap()[ph * 128:(ph + 1) * 128, pq0:pq0 + 512],
                ostg[pqsl].ap()))
            ost_done[pqsl] = sto3[pqsl].v

        def emit_sums_stage(pend, cur_qsl):
            pqsl, pv_part, ph_s, pq0_s = pend
            nc.vector.wait_ge(pe.h, av_done)
            v_ob = dv.inc(nc.vector.tensor_copy(obuf[pqsl].ap(), avp[pqsl][:]))
            nc.tensor.wait_ge(dv.h, pv_part)
            nc.tensor.wait_ge(dv.h, v_ob)
            ptree_mm[pqsl] = pe.inc(nc.tensor.matmul(
                avp[pqsl][0:1, 0:512], ones_bfc.ap(), ptree2[pqsl][:, 0:512],
                start=True, stop=True, skip_group_check=True))
            if rec_read[pqsl]:
                nc.vector.wait_ge(pe.h, rec_read[pqsl])
            nc.vector.wait_ge(pe.h, ptree_mm[pqsl])
            v_rec = dv.inc(nc.vector.reciprocal(rec2[pqsl][0:1, :],
                                                avp[pqsl][0:1, 0:512]))
            rec_ready[pqsl] = v_rec
            avfree[pqsl] = v_rec
            return (ph_s, pq0_s, pqsl, v_ob)

        ptree_mm = {0: 0, 1: 0}
        rec_ready = {0: 0, 1: 0}
        rec_read = {0: 0, 1: 0}
        vpart_slot = {0: 0, 1: 0}
        sumbank = {}

        def emit_vtrans(hh):
            psl = hh % 2
            nc.tensor.wait_ge(ldv[psl].h, ldv[psl].v)
            if vh_slot_free[psl]:
                nc.tensor.wait_ge(pe.h, vh_slot_free[psl])
            for kt0 in range(0, NT, 16):
                nb = min(16, NT - kt0)
                b = take_bank()
                bview = sg3[b][:].bitcast(BF16)
                for j in range(nb):
                    kt = kt0 + j
                    ins = nc.tensor.transpose(bview[:, j * 128:(j + 1) * 128],
                                              vTf_b[psl][:, kt * 128:(kt + 1) * 128],
                                              ident[:])
                    if j == nb - 1:
                        pe.inc(ins)
                nc.vector.wait_ge(pe.h, pe.v)
                ev = dv.inc(nc.vector.tensor_copy(vh2[psl][:, kt0 * 128:(kt0 + nb) * 128],
                                                  bview[:, 0:nb * 128]))
                bank_waits[b].append((dv, ev))

        def emit_norm(hh):
            psl = hh % 2
            nc.scalar.wait_ge(ldk[psl].h, ldk[psl].v)
            nc.scalar.wait_ge(ldq[psl].h, ldq[psl].v)
            if sq3_free[psl]:
                nc.scalar.wait_ge(pe.h, sq3_free[psl])
            for (srcb, col) in ((kT_b[psl], 1), (qT_b[psl], 0)):
                if col == 0:
                    nc.scalar.wait_ge(pe.h, sq3_free[psl])
                aq = ac.inc(nc.scalar.activation(sq3b[psl].ap(), srcb.ap(), AF.Square))
                nc.tensor.wait_ge(ac.h, aq)
                nc.vector.wait_ge(dv.h, dv.v)
                for tc in range(NQC):
                    b = take_bank()
                    pe.inc(nc.tensor.matmul(
                        sg3[b][0:1, 0:512], ones_bfc.ap(),
                        sq3b[psl][:, tc * 512:(tc + 1) * 512], start=True, stop=True))
                    nc.vector.wait_ge(pe.h, pe.v)
                    rv = dv.inc(nc.vector.reduce_max(mxp[0:1, tc:tc + 1],
                                                     sg3[b][0:1, 0:512], axis=AX.X))
                    bank_waits[b].append((dv, rv))
                sq3_free[psl] = pe.v
                nc.vector.wait_ge(dv.h, dv.v)
                dv.inc(nc.vector.reduce_max(mxs[0:1, col:col + 1], mxp[0:1, 0:NQC],
                                            axis=AX.X))
            nc.vector.wait_ge(dv.h, dv.v)
            v_m2 = dv.inc(nc.vector.tensor_tensor(mxs[0:1, 2:3], mxs[0:1, 0:1],
                                                  mxs[0:1, 1:2], ALU.mult))
            nc.scalar.wait_ge(dv.h, v_m2)
            a_l = ac.inc(nc.scalar.activation(mxs[0:1, 3:4], mxs[0:1, 2:3], AF.Ln,
                                              scale=1.002))
            nc.scalar.wait_ge(ac.h, a_l)
            a_c = ac.inc(nc.scalar.activation(mxs[0:1, 3:4], mxs[0:1, 3:4], AF.Exp,
                                              scale=LORA_SCALING))
            nc.scalar.wait_ge(ac.h, a_c)
            a_cn = ac.inc(nc.scalar.mul(mxs[0:1, 3:4], mxs[0:1, 3:4], -1.0))
            b = take_bank()
            nc.tensor.wait_ge(ac.h, a_cn)
            pv = pe.inc(nc.tensor.matmul(sg3[b][:, 0:1], ones_f32r.ap(),
                                         mxs[0:1, 3:4], start=True, stop=True))
            nc.vector.wait_ge(pe.h, pv)
            v_nc = dv.inc(nc.vector.tensor_copy(negc2[psl].ap(), sg3[b][:, 0:1]))
            bank_waits[b].append((dv, v_nc))
            negc_ready[psl] = v_nc

        ldk[0].dinc(sync.dma_start(kT_b[0].ap(), s_kT.ap()[0:128, :]))
        ldq[0].dinc(sync.dma_start(qT_b[0].ap(), s_qT.ap()[0:128, :]))
        ldv[0].dinc(sync.dma_start(vTf_b[0].ap(), s_vT.ap()[0:128, :]))
        emit_vtrans(0)
        emit_norm(0)

        for h in range(H):
            sl = h % 2
            if h + 1 < H:
                if h >= 1:
                    sync.wait_ge(pe.h, av_done)
                r0 = (h + 1) * 128
                ldk[(h + 1) % 2].dinc(sync.dma_start(kT_b[(h + 1) % 2].ap(),
                                                     s_kT.ap()[r0:r0 + 128, :]))
                ldq[(h + 1) % 2].dinc(sync.dma_start(qT_b[(h + 1) % 2].ap(),
                                                     s_qT.ap()[r0:r0 + 128, :]))
                ldv[(h + 1) % 2].dinc(sync.dma_start(vTf_b[(h + 1) % 2].ap(),
                                                     s_vT.ap()[r0:r0 + 128, :]))

            if h >= 1:
                emit_vtrans(h)
                emit_norm(h)

            s_done_cur = []
            s_bank_cur = []
            for qc in range(NQC):
                qsl = qc % 2
                q0 = qc * 512

                if pending_sums is not None:
                    fin2 = emit_sums_stage(pending_sums, qsl)
                    pending_sums = None
                else:
                    fin2 = None
                if pending_fin is not None:
                    assert pending_fin[2] == qsl, (h, qc, pending_fin)
                    emit_fin_stage(pending_fin, qsl)
                pending_fin = fin2

                def emit_S(g, dstv, dstb, qq0):
                    b = take_bank()
                    for j in range(2):
                        kt = 2 * g + j
                        ins = nc.tensor.matmul(
                            sg3[b][:, j * 512:(j + 1) * 512],
                            kT_b[sl][:, kt * 128:(kt + 1) * 128],
                            qT_b[sl][:, qq0:qq0 + 512],
                            start=True, stop=True, skip_group_check=True)
                        if j == 1:
                            pe.inc(ins)
                    dstv.append(pe.v)
                    dstb.append(b)

                if qc == 0:
                    s_done_cur, s_bank_cur = [], []
                    for g0 in range(min(3, NKG)):
                        emit_S(g0, s_done_cur, s_bank_cur, q0)
                s_done, s_bank = s_done_cur, s_bank_cur
                for g in range(NKG):
                    nc.scalar.wait_ge(pe.h, s_done[g])
                    if g == 0:
                        nc.scalar.wait_ge(pe.h, av_done_slot[qsl])
                        if vpart_slot[qsl]:
                            nc.scalar.wait_ge(dv.h, vpart_slot[qsl])
                        if qc == 0:
                            nc.scalar.wait_ge(dv.h, negc_ready[sl])
                    e = ac.inc(nc.scalar.activation(
                        pT2[qsl][:, g * 1024:(g + 1) * 1024], sg3[s_bank[g]][:],
                        AF.Exp, bias=negc2[sl].ap()))
                    bank_waits[s_bank[g]].append((ac, e))
                    nc.tensor.wait_ge(ac.h, e)
                    if g == 0:
                        if avfree[qsl]:
                            nc.tensor.wait_ge(dv.h, avfree[qsl])
                    if g + 3 < NKG:
                        emit_S(g + 3, s_done, s_bank, q0)
                    for j in range(2):
                        kt = 2 * g + j
                        pe.inc(nc.tensor.matmul(
                            avp[qsl][:], vh2[sl][:, kt * 128:(kt + 1) * 128],
                            pT2[qsl][:, kt * 512:(kt + 1) * 512],
                            start=(kt == 0), stop=(kt == NT - 1),
                            skip_group_check=True))
                    if g == (3 * NT // 4 - 1) // 2:
                        wq = NT * 256
                        if ptree_mm[qsl]:
                            nc.vector.wait_ge(pe.h, ptree_mm[qsl])
                        nc.vector.wait_ge(ac.h, e)
                        with nc.allow_low_precision(reason="bf16 tree partials (softmax sums; final 128-way sum in fp32 PSUM)"):
                            tree_a[0] = dv.inc(nc.vector.tensor_tensor(
                                ptree2[qsl][:, 0:wq // 2], pT2[qsl][:, 0:wq // 2],
                                pT2[qsl][:, wq:wq + wq // 2], ALU.add))
                if qc + 1 < NQC:
                    s_done_cur, s_bank_cur = [], []
                    for g0 in range(min(3, NKG)):
                        emit_S(g0, s_done_cur, s_bank_cur, (qc + 1) * 512)
                av_done = pe.v
                av_done_slot[qsl] = pe.v

                nc.vector.wait_ge(ac.h, ac.v)
                w = NT * 256
                with nc.allow_low_precision(reason="bf16 tree partials (softmax sums; final 128-way sum in fp32 PSUM)"):
                    v_part = dv.inc(nc.vector.tensor_tensor(
                        ptree2[qsl][:, w // 2:w], pT2[qsl][:, w // 2:w],
                        pT2[qsl][:, w + w // 2:2 * w], ALU.add))
                    while w > 512:
                        nc.vector.wait_ge(dv.h, v_part)
                        if tree_a[0]:
                            nc.vector.wait_ge(dv.h, tree_a[0])
                            tree_a[0] = 0
                        v_part = dv.inc(nc.vector.tensor_tensor(
                            ptree2[qsl][:, 0:w // 2], ptree2[qsl][:, 0:w // 2],
                            ptree2[qsl][:, w // 2:w], ALU.add))
                        w //= 2
                vpart_slot[qsl] = v_part
                pending_sums = (qsl, v_part, h, q0)
            vh_slot_free[sl] = av_done

        if pending_sums is not None:
            fin2 = emit_sums_stage(pending_sums, 1 - pending_sums[0])
        else:
            fin2 = None
        if pending_fin is not None:
            emit_fin_stage(pending_fin, pending_fin[2])
        if fin2 is not None:
            emit_fin_stage(fin2, fin2[2])

        sync.wait_ge(ldk[0].h, ldk[0].v)
        sync.wait_ge(ldk[1].h, ldk[1].v)
        sync.wait_ge(ldq[0].h, ldq[0].v)
        sync.wait_ge(ldq[1].h, ldq[1].v)
        sync.wait_ge(ldv[0].h, ldv[0].v)
        sync.wait_ge(ldv[1].h, ldv[1].v)
        nc.scalar.wait_ge(sto3[0].h, sto3[0].v)
        nc.scalar.wait_ge(sto3[1].h, sto3[1].v)

    PHASES.append(_phase3)
    es_p3.close()

    es_p4a = ExitStack()
    woT = es_p4a.enter_context(nc.sbuf_tensor("woT", [128, NCH * D], BF16))
    wbuf4 = [es_p4a.enter_context(nc.sbuf_tensor(f"wbuf4{i}", [128, D], F32)) for i in range(2)]
    wbf4 = es_p4a.enter_context(nc.sbuf_tensor("wbf4", [128, D], BF16))
    o_sp = [es_p4a.enter_context(nc.sbuf_tensor(f"o_sp{i}", [128, NCH * 512], BF16)) for i in range(2)]
    x_sp = [es_p4a.enter_context(nc.sbuf_tensor(f"x_sp{i}", [128, D], F32)) for i in range(2)]
    res4 = [es_p4a.enter_context(nc.sbuf_tensor(f"res4{i}", [128, 1024], F32)) for i in range(2)]
    oBs4 = es_p4a.enter_context(nc.sbuf_tensor("oBs4", [16, 512], BF16))
    bo_row = es_p4a.enter_context(nc.sbuf_tensor("bo_row", [1, D], BF16))
    bo_row32 = es_p4a.enter_context(nc.sbuf_tensor("bo_row32", [1, D], F32))
    pa = [es_p4a.enter_context(nc.psum_tensor(f"pa{i}", [128, 1024], F32)) for i in range(2)]
    pob = es_p4a.enter_context(nc.psum_tensor("pob", [128, 512], F32))
    ptr4 = es_p4a.enter_context(nc.psum_tensor("ptr4", [128, 1024], BF16))

    NSP = T // 512
    woT_v = woT.ap().rearrange("p (c n) -> p c n", n=D)
    so_v = s_o.ap().rearrange("(c p) t -> p c t", p=128)

    def _phase4(sync):
        ldw = [S("p4a_ldw0"), S("p4a_ldw1")]
        ldo = [S("p4a_ldo0"), S("p4a_ldo1")]
        ldx4 = [S("p4a_ldx0"), S("p4a_ldx1")]
        pe = S("p4a_pe")
        dv = S("p4a_dv")
        ac = S("p4a_ac")
        sto4 = [S("p4a_sto0"), S("p4a_sto1")]
        resfree = {0: 0, 1: 0}
        res_spill = {0: 0, 1: 0}
        cast_v = 0
        obev = 0
        osp_done = 0
        x_used = 0

        ldb = S("p4a_ldb")
        ldb.dinc(sync.dma_start(bo_row32.ap(), bo_d.ap()))
        nc.vector.wait_ge(ldb.h, ldb.v)
        dv.inc(nc.vector.tensor_copy(bo_row.ap(), bo_row32.ap()))

        ldw[0].dinc(sync.dma_start(wbuf4[0].ap(), wo_d.ap()[0:128, :]))
        for oc in range(NCH):
            sl = oc % 2
            if oc + 1 < NCH:
                if oc >= 1:
                    sync.wait_ge(dv.h, cast_v)
                ldw[(oc + 1) % 2].dinc(sync.dma_start(
                    wbuf4[(oc + 1) % 2].ap(), wo_d.ap()[(oc + 1) * 128:(oc + 2) * 128, :]))
            nc.vector.wait_ge(ldw[sl].h, ldw[sl].v)
            nc.vector.wait_ge(pe.h, pe.v)
            cast_v = dv.inc(nc.vector.tensor_copy(wbf4.ap(), wbuf4[sl].ap()))
            nc.tensor.wait_ge(dv.h, cast_v)
            for half in range(0, NCH, 8):
                nh = min(8, NCH - half)
                if half > 0 or oc > 0:
                    nc.tensor.wait_ge(dv.h, dv.v)
                for j in range(nh):
                    ins = nc.tensor.transpose(
                        ptr4[:, j * 128:(j + 1) * 128],
                        wbf4[:, (half + j) * 128:(half + j + 1) * 128], ident[:])
                    if j == nh - 1:
                        pe.inc(ins)
                nc.vector.wait_ge(pe.h, pe.v)
                dv.inc(nc.vector.tensor_copy(
                    woT_v[:, half:half + nh, oc * 128:(oc + 1) * 128],
                    ptr4[:, 0:nh * 128].rearrange("p (c n) -> p c n", n=128)))

        nc.tensor.wait_ge(dv.h, dv.v)
        ldo[0].dinc(sync.dma_start(
            o_sp[0].ap().rearrange("p (c n) -> p c n", n=512), so_v[:, :, 0:512]))
        ldx4[0].dinc(sync.dma_start(x_sp[0].ap(), x_d.ap()[0:128, :]))

        for sp in range(NSP):
            sl = sp % 2
            if sp + 1 < NSP:
                if sp >= 1:
                    sync.wait_ge(pe.h, osp_done)
                ldo[(sp + 1) % 2].dinc(sync.dma_start(
                    o_sp[(sp + 1) % 2].ap().rearrange("p (c n) -> p c n", n=512),
                    so_v[:, :, (sp + 1) * 512:(sp + 2) * 512]))

            nc.tensor.wait_ge(ldo[sl].h, ldo[sl].v)
            if obev:
                nc.tensor.wait_ge(ac.h, obev)
            for ic in range(NCH):
                pe.inc(nc.tensor.matmul(
                    pob[0:16, :], BoT[:, ic * 16:(ic + 1) * 16],
                    o_sp[sl][:, ic * 512:(ic + 1) * 512],
                    start=(ic == 0), stop=(ic == NCH - 1)))
            nc.scalar.wait_ge(pe.h, pe.v)
            obev = ac.inc(nc.scalar.activation(oBs4.ap(), pob[0:16, :], AF.Copy,
                                               scale=LORA_SCALING))

            for tt in range(4):
                t = sp * 4 + tt
                if t + 1 < NT:
                    if t >= 1:
                        sync.wait_ge(dv.h, x_used)
                    ldx4[(t + 1) % 2].dinc(sync.dma_start(
                        x_sp[(t + 1) % 2].ap(), x_d.ap()[(t + 1) * 128:(t + 2) * 128, :]))
                for hf in range(NHF):
                    psl = (t * NHF + hf) % 2
                    if resfree[psl]:
                        nc.tensor.wait_ge(dv.h, resfree[psl])
                    nc.tensor.wait_ge(ac.h, obev)
                    for ocs in range(2):
                        ocol = hf * 1024 + ocs * 512
                        for ic in range(NCH):
                            pe.inc(nc.tensor.matmul(
                                pa[psl][:, ocs * 512:(ocs + 1) * 512],
                                o_sp[sl][:, ic * 512 + tt * 128: ic * 512 + tt * 128 + 128],
                                woT[:, ic * D + ocol: ic * D + ocol + 512],
                                start=(ic == 0), stop=False, skip_group_check=True))
                        pe.inc(nc.tensor.matmul(
                            pa[psl][:, ocs * 512:(ocs + 1) * 512],
                            oBs4[0:16, tt * 128:(tt + 1) * 128],
                            ATo[0:16, ocol:ocol + 512],
                            start=False, stop=False, skip_group_check=True))
                        pe.inc(nc.tensor.matmul(
                            pa[psl][:, ocs * 512:(ocs + 1) * 512],
                            ones_bfr.ap(), bo_row[0:1, ocol:ocol + 512],
                            start=False, stop=True, skip_group_check=True))
                    nc.vector.wait_ge(pe.h, pe.v)
                    nc.vector.wait_ge(ldx4[t % 2].h, ldx4[t % 2].v)
                    if res_spill[psl]:
                        nc.vector.wait_ge(sto4[psl].h, res_spill[psl])
                    rv = dv.inc(nc.vector.tensor_tensor(
                        res4[psl].ap(), pa[psl][:],
                        x_sp[t % 2][:, hf * 1024:(hf + 1) * 1024], ALU.add))
                    resfree[psl] = rv
                    if hf == NHF - 1:
                        x_used = rv
                    nc.scalar.wait_ge(dv.h, rv)
                    sto4[psl].dinc(nc.scalar.dma_start(
                        out_d.ap()[t * 128:(t + 1) * 128, hf * 1024:(hf + 1) * 1024],
                        res4[psl].ap()))
                    res_spill[psl] = sto4[psl].v
            osp_done = pe.v

        sync.wait_ge(ldo[0].h, ldo[0].v)
        sync.wait_ge(ldo[1].h, ldo[1].v)
        sync.wait_ge(ldx4[0].h, ldx4[0].v)
        sync.wait_ge(ldx4[1].h, ldx4[1].v)
        sync.wait_ge(ldw[0].h, ldw[0].v)
        sync.wait_ge(ldw[1].h, ldw[1].v)
        nc.scalar.wait_ge(sto4[0].h, sto4[0].v)
        nc.scalar.wait_ge(sto4[1].h, sto4[1].v)

    PHASES.append(_phase4)
    es_p4a.close()

    es_p4b = ExitStack()
    swT = es_p4b.enter_context(nc.sbuf_tensor("swT", [128, NCH * D], BF16))
    fwb4 = [es_p4b.enter_context(nc.sbuf_tensor(f"fwb4{i}", [128, D], F32)) for i in range(2)]
    swsg = es_p4b.enter_context(nc.sbuf_tensor("swsg", [128, D], BF16))
    a_sp = [es_p4b.enter_context(nc.sbuf_tensor(f"a_sp{i}", [128, NCH * 512], BF16)) for i in range(2)]
    ord_b = [es_p4b.enter_context(nc.sbuf_tensor(f"ord{i}", [128, D], F32)) for i in range(2)]
    tf4 = [es_p4b.enter_context(nc.sbuf_tensor(f"tf4{i}", [128, 1024], F32)) for i in range(2)]
    res4b = [es_p4b.enter_context(nc.sbuf_tensor(f"res4b{i}", [128, 1024], F32)) for i in range(2)]
    al_bc = es_p4b.enter_context(nc.sbuf_tensor("al_bc", [128, D], F32))
    al_row = es_p4b.enter_context(nc.sbuf_tensor("al_row", [1, D], F32))
    fb_row = es_p4b.enter_context(nc.sbuf_tensor("fb_row", [1, D], BF16))
    fb_row32 = es_p4b.enter_context(nc.sbuf_tensor("fb_row32", [1, D], F32))
    rs4 = es_p4b.enter_context(nc.sbuf_tensor("rs4", [128, 4], F32))
    sw_scr = es_p4b.enter_context(nc.sbuf_tensor("sw_scr", [128, D], F32))
    pf = [es_p4b.enter_context(nc.psum_tensor(f"pf{i}", [128, 1024], F32)) for i in range(2)]
    ptr4b = es_p4b.enter_context(nc.psum_tensor("ptr4b", [128, 1024], BF16))
    pbc4 = es_p4b.enter_context(nc.psum_tensor("pbc4", [128, 512], F32))

    swT_v = swT.ap().rearrange("p (c n) -> p c n", n=D)
    sa_v = s_a.ap().rearrange("(c p) t -> p c t", p=128)

    def _phase5(sync):
        ldw = [S("p4b_ldw0"), S("p4b_ldw1")]
        lda = [S("p4b_lda0"), S("p4b_lda1")]
        ldr = [S("p4b_ldr0"), S("p4b_ldr1")]
        pe = S("p4b_pe")
        dv = S("p4b_dv")
        ac = S("p4b_ac")
        sto5 = [S("p4b_sto0"), S("p4b_sto1")]
        resfree = {0: 0, 1: 0}
        res_spill = {0: 0, 1: 0}
        sg_v = 0
        last_res_v = 0
        asp_done = 0
        o_used = 0

        ldb2 = S("p4b_ldb")
        ldb2.dinc(sync.dma_start(al_row.ap(), al_d.ap()))
        ldb2.dinc(sync.dma_start(fb_row32.ap(), fb_d.ap()))
        nc.vector.wait_ge(ldb2.h, ldb2.v)
        dv.inc(nc.vector.tensor_copy(fb_row.ap(), fb_row32.ap()))
        nc.tensor.wait_ge(ldb2.h, ldb2.v)
        for j in range(D // 512):
            if j > 0:
                nc.tensor.wait_ge(dv.h, dv.v)
            pe.inc(nc.tensor.matmul(pbc4[:], ones_f32r.ap(),
                                    al_row[0:1, j * 512:(j + 1) * 512],
                                    start=True, stop=True))
            nc.vector.wait_ge(pe.h, pe.v)
            dv.inc(nc.vector.tensor_copy(al_bc[:, j * 512:(j + 1) * 512], pbc4[:]))

        ldw[0].dinc(sync.dma_start(fwb4[0].ap(), fw_d.ap()[0:128, :]))
        for oc in range(NCH):
            sl = oc % 2
            if oc + 1 < NCH:
                if oc >= 1:
                    sync.wait_ge(ac.h, sg_v)
                    sync.wait_ge(dv.h, last_res_v)
                ldw[(oc + 1) % 2].dinc(sync.dma_start(
                    fwb4[(oc + 1) % 2].ap(), fw_d.ap()[(oc + 1) * 128:(oc + 2) * 128, :]))
            nc.vector.wait_ge(ldw[sl].h, ldw[sl].v)
            v0 = dv.inc(nc.vector.reduce_sum(rs4[:, 0:1], fwb4[sl].ap(), axis=AX.X))
            nc.vector.wait_ge(dv.h, v0)
            v1 = dv.inc(nc.vector.tensor_scalar_mul(rs4[:, 1:2], rs4[:, 0:1], INV_D))
            nc.vector.wait_ge(dv.h, v1)
            v2 = dv.inc(nc.vector.tensor_scalar(
                sw_scr.ap(), fwb4[sl].ap(), rs4[:, 1:2], None, ALU.subtract))
            nc.vector.wait_ge(dv.h, v2)
            v3 = dv.inc(nc.vector.reduce_sum(rs4[:, 2:3], sw_scr.ap(), axis=AX.X))
            nc.vector.wait_ge(dv.h, v3)
            v4 = dv.inc(nc.vector.tensor_scalar(
                rs4[:, 3:4], rs4[:, 2:3], NEG_INV_D, rs4[:, 1:2],
                ALU.mult, ALU.subtract))
            last_res_v = v2
            nc.scalar.wait_ge(dv.h, v4)
            nc.scalar.wait_ge(pe.h, pe.v)
            sg_v = ac.inc(nc.scalar.activation(swsg.ap(), fwb4[sl].ap(), AF.Sign,
                                               bias=rs4[:, 3:4]))
            nc.tensor.wait_ge(ac.h, sg_v)
            for half in range(0, NCH, 8):
                nh = min(8, NCH - half)
                if half > 0 or oc > 0:
                    nc.tensor.wait_ge(dv.h, dv.v)
                for j in range(nh):
                    ins = nc.tensor.transpose(
                        ptr4b[:, j * 128:(j + 1) * 128],
                        swsg[:, (half + j) * 128:(half + j + 1) * 128], ident[:])
                    if j == nh - 1:
                        pe.inc(ins)
                nc.vector.wait_ge(pe.h, pe.v)
                dv.inc(nc.vector.tensor_copy(
                    swT_v[:, half:half + nh, oc * 128:(oc + 1) * 128],
                    ptr4b[:, 0:nh * 128].rearrange("p (c n) -> p c n", n=128)))

        nc.tensor.wait_ge(dv.h, dv.v)
        lda[0].dinc(sync.dma_start(
            a_sp[0].ap().rearrange("p (c n) -> p c n", n=512), sa_v[:, :, 0:512]))
        ldr[0].dinc(sync.dma_start(ord_b[0].ap(), out_d.ap()[0:128, :]))

        NSP2 = T // 512
        for sp in range(NSP2):
            sl = sp % 2
            if sp + 1 < NSP2:
                if sp >= 1:
                    sync.wait_ge(pe.h, asp_done)
                lda[(sp + 1) % 2].dinc(sync.dma_start(
                    a_sp[(sp + 1) % 2].ap().rearrange("p (c n) -> p c n", n=512),
                    sa_v[:, :, (sp + 1) * 512:(sp + 2) * 512]))
            nc.tensor.wait_ge(lda[sl].h, lda[sl].v)
            for tt in range(4):
                t = sp * 4 + tt
                if t + 1 < NT:
                    if t >= 1:
                        sync.wait_ge(dv.h, o_used)
                    ldr[(t + 1) % 2].dinc(sync.dma_start(
                        ord_b[(t + 1) % 2].ap(), out_d.ap()[(t + 1) * 128:(t + 2) * 128, :]))
                for hf in range(NHF):
                    psl = (t * NHF + hf) % 2
                    if resfree[psl]:
                        nc.tensor.wait_ge(dv.h, resfree[psl])
                    for ocs in range(2):
                        ocol = hf * 1024 + ocs * 512
                        for ic in range(NCH):
                            pe.inc(nc.tensor.matmul(
                                pf[psl][:, ocs * 512:(ocs + 1) * 512],
                                a_sp[sl][:, ic * 512 + tt * 128: ic * 512 + tt * 128 + 128],
                                swT[:, ic * D + ocol: ic * D + ocol + 512],
                                start=(ic == 0), stop=False, skip_group_check=True))
                        pe.inc(nc.tensor.matmul(
                            pf[psl][:, ocs * 512:(ocs + 1) * 512],
                            ones_bfr.ap(), fb_row[0:1, ocol:ocol + 512],
                            start=False, stop=True, skip_group_check=True))
                    nc.vector.wait_ge(pe.h, pe.v)
                    nc.vector.wait_ge(ldr[t % 2].h, ldr[t % 2].v)
                    v_tf = dv.inc(nc.vector.tensor_tensor(
                        tf4[psl].ap(), pf[psl][:],
                        al_bc[:, hf * 1024:(hf + 1) * 1024], ALU.mult))
                    resfree[psl] = v_tf
                    nc.vector.wait_ge(dv.h, v_tf)
                    if res_spill[psl]:
                        nc.vector.wait_ge(sto5[psl].h, res_spill[psl])
                    rv2 = dv.inc(nc.vector.tensor_tensor(
                        res4b[psl].ap(), tf4[psl].ap(),
                        ord_b[t % 2][:, hf * 1024:(hf + 1) * 1024], ALU.add))
                    if hf == NHF - 1:
                        o_used = rv2
                    nc.scalar.wait_ge(dv.h, rv2)
                    sto5[psl].dinc(nc.scalar.dma_start(
                        out_d.ap()[t * 128:(t + 1) * 128, hf * 1024:(hf + 1) * 1024],
                        res4b[psl].ap()))
                    res_spill[psl] = sto5[psl].v
            asp_done = pe.v

        sync.wait_ge(lda[0].h, lda[0].v)
        sync.wait_ge(lda[1].h, lda[1].v)
        sync.wait_ge(ldr[0].h, ldr[0].v)
        sync.wait_ge(ldr[1].h, ldr[1].v)
        nc.scalar.wait_ge(sto5[0].h, sto5[0].v)
        nc.scalar.wait_ge(sto5[1].h, sto5[1].v)
        sync.wait_ge(sto5[0].h, sto5[0].v)
        sync.wait_ge(sto5[1].h, sto5[1].v)

    PHASES.append(_phase5)
    es_p4b.close()

    def _join():
        engines = [nc.tensor, nc.vector, nc.scalar, nc.gpsimd, nc.sync]
        for eng in engines:
            for sm in sems.values():
                if sm.v:
                    eng.wait_ge(sm.h, sm.v)

    with nc.Block() as block:
        @block.sync
        def _(sync):
            for pi, f in enumerate(PHASES):
                f(sync)
                if pi + 1 < len(PHASES):
                    _join()

    nc.finalize()
    return nc


def make_in_map(b, x, inputs):
    def row(v):
        return np.asarray(v, dtype=np.float32).reshape(1, -1)
    return {
        "x": x[b],
        "wq": inputs["L0_attn_Wq"], "wk": inputs["L0_attn_Wk"],
        "wv": inputs["L0_attn_Wv"], "wo": inputs["L0_attn_Wo"],
        "bq": inputs["L0_attn_bq"], "bk": inputs["L0_attn_bk"],
        "bv": inputs["L0_attn_bv"], "bo": row(inputs["L0_attn_bo"]),
        "Blq": inputs["L0_attn_Bq"], "Blk": inputs["L0_attn_Bk"],
        "Blv": inputs["L0_attn_Bv"], "Blo": inputs["L0_attn_Bo"],
        "Alq": inputs["L0_attn_Aq"], "Alk": inputs["L0_attn_Ak"],
        "Alv": inputs["L0_attn_Av"], "Alo": inputs["L0_attn_Ao"],
        "gm": row(inputs["L0_ffn_gamma"]), "bt": row(inputs["L0_ffn_beta"]),
        "fw": inputs["L0_ffn_weight"], "fb": row(inputs["L0_ffn_bias"]),
        "al": row(inputs["L0_ffn_alpha"]),
    }


def kernel(**inputs):
    x = np.asarray(inputs["L0_attn_x"], dtype=np.float32)
    B, T, D = x.shape
    nc = build_nc(T, D)
    in_maps = [make_in_map(b, x, inputs) for b in range(B)]
    res = run_bass_kernel_spmd(nc, in_maps, core_ids=list(range(B)))
    out = np.stack([np.asarray(res.results[b]["out"]) for b in range(B)], axis=0)
    return out.astype(np.float32)


# version 22
# speedup vs baseline: 1.0394x; 1.0394x over previous
import numpy as np
import ml_dtypes
from contextlib import ExitStack

import concourse.bass as bass
import concourse.mybir as mybir
from concourse.bass_utils import run_bass_kernel_spmd

F32 = mybir.dt.float32
BF16 = mybir.dt.bfloat16
AF = mybir.ActivationFunctionType
ALU = mybir.AluOpType
AX = mybir.AxisListType

EPS = 1e-5
R = 16
LORA_SCALING = 0.5


class Sem:

    def __init__(self, nc, name):
        self.h = nc.alloc_semaphore(name)
        self.v = 0

    def inc(self, instr, k=1):
        instr.then_inc(self.h, k)
        self.v += k
        return self.v

    def dinc(self, instr):
        return self.inc(instr, 16)


def build_nc(T, D):
    HD = 128
    H = D // HD
    NCH = D // 128
    NT = T // 128
    NQC = T // 512
    NKG = NT // 2
    SCALE = float(HD) ** -0.5
    INV_D = 1.0 / D
    NEG_INV_D = -1.0 / D
    NHF = D // 1024

    assert T % 512 == 0 and NT % 2 == 0 and D % 1024 == 0

    nc = bass.Bass("TRN2", target_bir_lowering=False)

    x_d = nc.dram_tensor("x", [T, D], F32, kind="ExternalInput")
    wq_d = nc.dram_tensor("wq", [D, D], F32, kind="ExternalInput")
    wk_d = nc.dram_tensor("wk", [D, D], F32, kind="ExternalInput")
    wv_d = nc.dram_tensor("wv", [D, D], F32, kind="ExternalInput")
    wo_d = nc.dram_tensor("wo", [D, D], F32, kind="ExternalInput")
    bq_d = nc.dram_tensor("bq", [D], F32, kind="ExternalInput")
    bk_d = nc.dram_tensor("bk", [D], F32, kind="ExternalInput")
    bv_d = nc.dram_tensor("bv", [D], F32, kind="ExternalInput")
    bo_d = nc.dram_tensor("bo", [1, D], F32, kind="ExternalInput")
    Bq_d = nc.dram_tensor("Blq", [R, D], F32, kind="ExternalInput")
    Bk_d = nc.dram_tensor("Blk", [R, D], F32, kind="ExternalInput")
    Bv_d = nc.dram_tensor("Blv", [R, D], F32, kind="ExternalInput")
    Bo_d = nc.dram_tensor("Blo", [R, D], F32, kind="ExternalInput")
    Aq_d = nc.dram_tensor("Alq", [D, R], F32, kind="ExternalInput")
    Ak_d = nc.dram_tensor("Alk", [D, R], F32, kind="ExternalInput")
    Av_d = nc.dram_tensor("Alv", [D, R], F32, kind="ExternalInput")
    Ao_d = nc.dram_tensor("Alo", [D, R], F32, kind="ExternalInput")
    gm_d = nc.dram_tensor("gm", [1, D], F32, kind="ExternalInput")
    bt_d = nc.dram_tensor("bt", [1, D], F32, kind="ExternalInput")
    fw_d = nc.dram_tensor("fw", [D, D], F32, kind="ExternalInput")
    fb_d = nc.dram_tensor("fb", [1, D], F32, kind="ExternalInput")
    al_d = nc.dram_tensor("al", [1, D], F32, kind="ExternalInput")
    out_d = nc.dram_tensor("out", [T, D], F32, kind="ExternalOutput")

    s_qT = nc.dram_tensor("s_qT", [D, T], BF16, kind="Internal")
    s_kT = nc.dram_tensor("s_kT", [D, T], BF16, kind="Internal")
    s_vT = nc.dram_tensor("s_vT", [D, T], BF16, kind="Internal")
    s_a = nc.dram_tensor("s_a", [D, T], BF16, kind="Internal")
    s_o = nc.dram_tensor("s_o", [D, T], BF16, kind="Internal")

    eye_d = nc.inline_tensor(np.eye(128, dtype=ml_dtypes.bfloat16), name="eye128")

    CONST_VALS = [EPS, -0.5, LORA_SCALING, INV_D, NEG_INV_D, SCALE, -1.0, 1.002]
    cst_tile = nc.alloc_sbuf_tensor("cstcols", [128, len(CONST_VALS)], F32)
    for i, v in enumerate(CONST_VALS):
        nc.const_aps.register(F32, float(v), cst_tile.ap()[:, i:i + 1])

    ident = nc.alloc_sbuf_tensor("ident", [128, 128], BF16)
    ones_f32r = nc.alloc_sbuf_tensor("ones_f32r", [1, 128], F32)
    ones_bfr = nc.alloc_sbuf_tensor("ones_bfr", [1, 128], BF16)
    ones_bfc = nc.alloc_sbuf_tensor("ones_bfc", [128, 1], BF16)
    ATqkv = nc.alloc_sbuf_tensor("ATqkv", [128, D], BF16)
    ATo = nc.alloc_sbuf_tensor("ATo", [16, D], BF16)
    BsT = nc.alloc_sbuf_tensor("BsT", [128, NCH * 80], BF16)
    BoT = nc.alloc_sbuf_tensor("BoT", [128, NCH * 16], BF16)
    bqs_c = nc.alloc_sbuf_tensor("bqs_c", [128, NCH], F32)
    bk_c = nc.alloc_sbuf_tensor("bk_c", [128, NCH], F32)
    bv_c = nc.alloc_sbuf_tensor("bv_c", [128, NCH], F32)

    sems = {}

    def S(name):
        if name not in sems:
            sems[name] = Sem(nc, name)
        return sems[name]

    band = {0: 0, 1: 32, 2: 64}
    PHASES = []

    es_xn = ExitStack()
    xn_fm = es_xn.enter_context(nc.sbuf_tensor("xn_fm", [128, NCH * T], BF16))

    es_p1 = ExitStack()
    gb_g = es_p1.enter_context(nc.sbuf_tensor("gb_g", [128, D], F32))
    gb_b = es_p1.enter_context(nc.sbuf_tensor("gb_b", [128, D], F32))
    xf = [es_p1.enter_context(nc.sbuf_tensor(f"xf{i}", [128, D], F32)) for i in range(2)]
    xn32 = es_p1.enter_context(nc.sbuf_tensor("xn32", [128, D], F32))
    xnbf = es_p1.enter_context(nc.sbuf_tensor("xnbf", [128, D], BF16))
    abf = es_p1.enter_context(nc.sbuf_tensor("abf", [128, D], BF16))
    a_acc = es_p1.enter_context(nc.sbuf_tensor("a_acc", [128, NCH * 256], BF16))
    stat = es_p1.enter_context(nc.sbuf_tensor("stat", [128, 8], F32))
    ptrA = es_p1.enter_context(nc.psum_tensor("ptrA", [128, 1024], BF16))
    ptrB = es_p1.enter_context(nc.psum_tensor("ptrB", [128, 1024], BF16))

    es_b0 = ExitStack()
    braw1 = es_b0.enter_context(nc.sbuf_tensor("braw1", [16, D // 2], F32))
    brawb1 = es_b0.enter_context(nc.sbuf_tensor("brawb1", [16, D // 2], BF16))
    araw1 = es_b0.enter_context(nc.sbuf_tensor("araw1", [128, (NCH // 2) * 16], F32))
    arawb1 = es_b0.enter_context(nc.sbuf_tensor("arawb1", [128, (NCH // 2) * 16], BF16))
    p_t16 = es_b0.enter_context(nc.psum_tensor("p_t16", [128, 1024], BF16))

    def _phase0(sync):
        ld = S("b0_ld")
        gp = S("b0_gp")
        pe = S("b0_pe")
        dv = S("b0_dv")
        ac = S("b0_ac")

        for i, v in enumerate(CONST_VALS):
            nc.gpsimd.memset(cst_tile.ap()[:, i:i + 1], float(v))
        nc.gpsimd.memset(ones_f32r.ap(), 1.0)
        nc.gpsimd.memset(ones_bfr.ap(), 1.0)
        nc.gpsimd.memset(ones_bfc.ap(), 1.0)
        nc.gpsimd.memset(BsT.ap(), 0.0)
        gp.inc(nc.gpsimd.memset(ATqkv.ap(), 0.0))

        ld.dinc(sync.dma_start(ident.ap(), eye_d.ap()))
        for c in range(NCH):
            ld.dinc(sync.dma_start(bqs_c.ap()[:, c:c + 1], bq_d.ap()[c * 128:(c + 1) * 128]))
            ld.dinc(sync.dma_start(bk_c.ap()[:, c:c + 1], bk_d.ap()[c * 128:(c + 1) * 128]))
            ld.dinc(sync.dma_start(bv_c.ap()[:, c:c + 1], bv_d.ap()[c * 128:(c + 1) * 128]))
        nc.scalar.wait_ge(ld.h, ld.v)
        nc.scalar.wait_ge(gp.h, gp.v)
        ac.inc(nc.scalar.mul(bqs_c.ap(), bqs_c.ap(), SCALE))

        ld2 = S("b0_ld2")
        nc.tensor.wait_ge(ld.h, ld.v)
        HCH = NCH // 2
        for i, bd in enumerate([Bq_d, Bk_d, Bv_d, Bo_d]):
            for hf in range(2):
                if dv.v:
                    sync.wait_ge(dv.h, dv.v)
                lv = ld2.dinc(sync.dma_start(
                    braw1.ap(), bd.ap()[:, hf * (D // 2):(hf + 1) * (D // 2)]))
                nc.vector.wait_ge(ld2.h, lv)
                if ld2.v > 16:
                    nc.vector.wait_ge(pe.h, pe.v)
                cv = dv.inc(nc.vector.tensor_copy(brawb1.ap(), braw1.ap()))
                nc.tensor.wait_ge(dv.h, cv)
                nc.tensor.wait_ge(gp.h, gp.v)
                for cc in range(HCH):
                    c = hf * HCH + cc
                    ins = nc.tensor.transpose(p_t16[:, cc * 16:(cc + 1) * 16],
                                              brawb1[:, cc * 128:(cc + 1) * 128],
                                              ident[0:16, 0:16])
                    if cc == HCH - 1:
                        pe.inc(ins)
                nc.vector.wait_ge(pe.h, pe.v)
                for cc in range(HCH):
                    c = hf * HCH + cc
                    if i < 3:
                        dst = BsT[:, c * 80 + band[i]: c * 80 + band[i] + 16]
                    else:
                        dst = BoT[:, c * 16:(c + 1) * 16]
                    ins = nc.vector.tensor_copy(dst, p_t16[:, cc * 16:(cc + 1) * 16])
                    if cc == HCH - 1:
                        dv.inc(ins)
                nc.tensor.wait_ge(dv.h, dv.v)

        for i, ad in enumerate([Aq_d, Ak_d, Av_d, Ao_d]):
            for hf in range(2):
                c0 = hf * HCH
                sync.wait_ge(dv.h, dv.v)
                lv = ld2.dinc(sync.dma_start(
                    araw1.ap().rearrange("p (c r) -> p c r", r=R),
                    ad.ap().rearrange("(c p) r -> p c r", p=128)[:, c0:c0 + HCH, :]))
                nc.vector.wait_ge(ld2.h, lv)
                nc.vector.wait_ge(pe.h, pe.v)
                cv = dv.inc(nc.vector.tensor_copy(arawb1.ap(), araw1.ap()))
                nc.tensor.wait_ge(dv.h, cv)
                for cc in range(HCH):
                    ins = nc.tensor.transpose(p_t16[0:16, cc * 128:(cc + 1) * 128],
                                              arawb1[:, cc * 16:(cc + 1) * 16],
                                              ident[:])
                    if cc == HCH - 1:
                        pe.inc(ins)
                nc.vector.wait_ge(pe.h, pe.v)
                for cc in range(HCH):
                    c = c0 + cc
                    if i < 3:
                        dst = ATqkv[band[i]:band[i] + 16, c * 128:(c + 1) * 128]
                    else:
                        dst = ATo[0:16, c * 128:(c + 1) * 128]
                    ins = nc.vector.tensor_copy(dst, p_t16[0:16, cc * 128:(cc + 1) * 128])
                    if cc == HCH - 1:
                        dv.inc(ins)
                nc.tensor.wait_ge(dv.h, dv.v)

    PHASES.append(_phase0)
    es_b0.close()


    xn_v = xn_fm.ap().rearrange("p (c n) -> p c n", n=T)
    aac_v = a_acc.ap().rearrange("p (c n) -> p c n", n=256)

    def _phase1(sync):
        ldx = [S("p1_ldx0"), S("p1_ldx1")]
        dv = S("p1_dv")
        ac = S("p1_ac")
        pe = S("p1_pe")
        sta = S("p1_sta")
        evA = S("p1_evA")
        evB = S("p1_evB")
        xf_last = {}
        abf_free = [0]
        xnbf_free = [0]

        def stats_pass(src, scratch, act_dv_wait=None, act_pe_wait=None,
                       dv_wait=None, mid=None):
            if dv_wait is not None:
                nc.vector.wait_ge(dv.h, dv_wait)
            v_sum = dv.inc(nc.vector.reduce_sum(stat[:, 0:1], src, axis=AX.X))
            nc.vector.wait_ge(dv.h, v_sum)
            v_mu = dv.inc(nc.vector.tensor_scalar_mul(stat[:, 2:3], stat[:, 0:1], INV_D))
            nc.vector.wait_ge(dv.h, v_mu)
            v_nm = dv.inc(nc.vector.tensor_scalar_mul(stat[:, 3:4], stat[:, 2:3], -1.0))
            if act_pe_wait is not None:
                nc.scalar.wait_ge(pe.h, act_pe_wait)
            if act_dv_wait is not None:
                nc.scalar.wait_ge(dv.h, act_dv_wait)
            nc.scalar.wait_ge(dv.h, v_nm)
            a_sq = ac.inc(nc.scalar.activation(scratch, src, AF.Square,
                                               bias=stat[:, 3:4],
                                               accum_out=stat[:, 1:2]))
            if mid is not None:
                mid()
            nc.vector.wait_ge(ac.h, a_sq)
            v_var = dv.inc(nc.vector.tensor_scalar_mul(stat[:, 4:5], stat[:, 1:2], INV_D))
            nc.scalar.wait_ge(dv.h, v_var)
            a_ln = ac.inc(nc.scalar.activation(stat[:, 7:8], stat[:, 4:5], AF.Ln,
                                               bias=EPS))
            nc.scalar.wait_ge(ac.h, a_ln)
            a_r = ac.inc(nc.scalar.activation(stat[:, 5:6], stat[:, 7:8], AF.Exp,
                                              scale=-0.5))
            nc.vector.wait_ge(ac.h, a_r)
            v_nmr = dv.inc(nc.vector.tensor_scalar(
                stat[:, 6:7], stat[:, 2:3], stat[:, 5:6], -1.0, ALU.mult, ALU.mult))
            return v_nmr, a_sq

        gld = S("p1_gld")
        gld.dinc(sync.dma_start(xf[0].ap()[0:1, :], gm_d.ap()))
        gld.dinc(sync.dma_start(xf[1].ap()[0:1, :], bt_d.ap()))
        nc.tensor.wait_ge(gld.h, gld.v)
        for src_row, dst in ((xf[0], gb_g), (xf[1], gb_b)):
            for j in range(D // 512):
                if dv.v:
                    nc.tensor.wait_ge(dv.h, dv.v)
                pe.inc(nc.tensor.matmul(
                    ptrA[:, 0:1024].bitcast(F32), ones_f32r.ap(),
                    src_row.ap()[0:1, j * 512:(j + 1) * 512], start=True, stop=True))
                nc.vector.wait_ge(pe.h, pe.v)
                dv.inc(nc.vector.tensor_copy(dst[:, j * 512:(j + 1) * 512],
                                             ptrA[:, 0:1024].bitcast(F32)))
        nc.vector.wait_ge(dv.h, dv.v)

        sync.wait_ge(pe.h, pe.v)
        ldx[0].dinc(sync.dma_start(xf[0].ap(), x_d.ap()[0:128, :]))

        for t in range(NT):
            sl = t % 2
            if t + 1 < NT:
                if t >= 1:
                    pdv, pac = xf_last[t - 1]
                    sync.wait_ge(dv.h, pdv)
                    sync.wait_ge(ac.h, pac)
                ldx[(t + 1) % 2].dinc(sync.dma_start(
                    xf[(t + 1) % 2].ap(), x_d.ap()[(t + 1) * 128:(t + 2) * 128, :]))

            nc.vector.wait_ge(ldx[sl].h, ldx[sl].v)
            nc.scalar.wait_ge(ldx[sl].h, ldx[sl].v)
            v_nmr, a_sq1 = stats_pass(xf[sl].ap(), abf.ap(),
                                      act_pe_wait=abf_free[0])
            nc.vector.wait_ge(dv.h, v_nmr)
            v_xn = dv.inc(nc.vector.tensor_scalar(
                xn32.ap(), xf[sl].ap(), stat[:, 5:6], stat[:, 6:7],
                ALU.mult, ALU.add))
            xf_last[t] = (v_xn, a_sq1)
            xnbf_cp = [0]

            def _cast():
                nc.scalar.wait_ge(ac.h, ac.v)
                xnbf_cp[0] = ac.inc(nc.scalar.activation(xnbf.ap(), xn32.ap(),
                                                         AF.Copy))
            v_nmr2, a_sq2 = stats_pass(xn32.ap(), xnbf.ap(), act_dv_wait=v_xn,
                                       act_pe_wait=xnbf_free[0], dv_wait=v_xn,
                                       mid=_cast)
            nc.vector.wait_ge(dv.h, v_xn)
            nc.vector.wait_ge(dv.h, v_nmr2)
            v_u = dv.inc(nc.vector.tensor_scalar(
                xn32.ap(), xn32.ap(), stat[:, 5:6], stat[:, 6:7],
                ALU.mult, ALU.add))
            nc.vector.wait_ge(dv.h, v_u)
            v_g = dv.inc(nc.vector.tensor_tensor(xn32.ap(), xn32.ap(), gb_g.ap(), ALU.mult))
            nc.vector.wait_ge(dv.h, v_g)
            v_b = dv.inc(nc.vector.tensor_tensor(xn32.ap(), xn32.ap(), gb_b.ap(), ALU.add))
            nc.scalar.wait_ge(dv.h, v_b)
            a_s = ac.inc(nc.scalar.activation(abf.ap(), xn32.ap(), AF.Sign))

            nhalf = min(8, NCH)
            nc.tensor.wait_ge(ac.h, xnbf_cp[0])
            nc.tensor.wait_ge(evA.h, evA.v)
            for j in range(nhalf):
                ins = nc.tensor.transpose(ptrA[:, j * 128:(j + 1) * 128],
                                          xnbf[:, j * 128:(j + 1) * 128], ident[:])
                if j == nhalf - 1:
                    pe.inc(ins)
            nc.scalar.wait_ge(pe.h, pe.v)
            evA.inc(nc.scalar.activation(
                xn_v[:, 0:nhalf, t * 128:(t + 1) * 128],
                ptrA[:].rearrange("p (c n) -> p c n", n=128)[:, 0:nhalf, :],
                AF.Copy))
            if NCH > 8:
                nc.tensor.wait_ge(evB.h, evB.v)
                for j in range(8, NCH):
                    ins = nc.tensor.transpose(ptrB[:, (j - 8) * 128:(j - 7) * 128],
                                              xnbf[:, j * 128:(j + 1) * 128], ident[:])
                    if j == NCH - 1:
                        pe.inc(ins)
                nc.scalar.wait_ge(pe.h, pe.v)
                evB.inc(nc.scalar.activation(
                    xn_v[:, 8:NCH, t * 128:(t + 1) * 128],
                    ptrB[:].rearrange("p (c n) -> p c n", n=128)[:, 0:NCH - 8, :],
                    AF.Copy))
            xnbf_free[0] = pe.v

            nc.tensor.wait_ge(ac.h, a_s)
            nc.tensor.wait_ge(evA.h, evA.v)
            for j in range(nhalf):
                ins = nc.tensor.transpose(ptrA[:, j * 128:(j + 1) * 128],
                                          abf[:, j * 128:(j + 1) * 128], ident[:])
                if j == nhalf - 1:
                    pe.inc(ins)
            nc.scalar.wait_ge(pe.h, pe.v)
            if t % 2 == 0 and t >= 2:
                nc.scalar.wait_ge(sta.h, sta.v)
            evA.inc(nc.scalar.activation(
                aac_v[:, 0:nhalf, (t % 2) * 128:(t % 2) * 128 + 128],
                ptrA[:].rearrange("p (c n) -> p c n", n=128)[:, 0:nhalf, :],
                AF.Copy))
            if NCH > 8:
                nc.tensor.wait_ge(evB.h, evB.v)
                for j in range(8, NCH):
                    ins = nc.tensor.transpose(ptrB[:, (j - 8) * 128:(j - 7) * 128],
                                              abf[:, j * 128:(j + 1) * 128], ident[:])
                    if j == NCH - 1:
                        pe.inc(ins)
                nc.scalar.wait_ge(pe.h, pe.v)
                evB.inc(nc.scalar.activation(
                    aac_v[:, 8:NCH, (t % 2) * 128:(t % 2) * 128 + 128],
                    ptrB[:].rearrange("p (c n) -> p c n", n=128)[:, 0:NCH - 8, :],
                    AF.Copy))

            abf_free[0] = pe.v

            if t % 2 == 1:
                nc.scalar.wait_ge(evA.h, evA.v)
                if NCH > 8:
                    nc.scalar.wait_ge(evB.h, evB.v)
                sta.dinc(nc.scalar.dma_start(
                    s_a.ap().rearrange("(c p) n -> p c n", p=128)[:, :,
                                                                 (t - 1) * 128:(t + 1) * 128],
                    aac_v))

        sync.wait_ge(ldx[0].h, ldx[0].v)
        sync.wait_ge(ldx[1].h, ldx[1].v)
        nc.scalar.wait_ge(sta.h, sta.v)

    PHASES.append(_phase1)
    es_p1.close()

    es_p2 = ExitStack()
    wbuf = [es_p2.enter_context(nc.sbuf_tensor(f"wbuf{i}", [128, D], F32)) for i in range(2)]
    wbf = [es_p2.enter_context(nc.sbuf_tensor(f"wbf{i}", [128, D], BF16)) for i in range(2)]
    wT = [es_p2.enter_context(nc.sbuf_tensor(f"wT{i}", [128, D], BF16)) for i in range(2)]
    stg = [es_p2.enter_context(nc.sbuf_tensor(f"stg{i}", [128, 1536], BF16)) for i in range(2)]
    xBs = es_p2.enter_context(nc.sbuf_tensor("xBs", [128, T], BF16))
    pgA = es_p2.enter_context(nc.psum_tensor("pgA", [128, 1536], F32))
    pgB = es_p2.enter_context(nc.psum_tensor("pgB", [128, 1536], F32))
    ptrW = es_p2.enter_context(nc.psum_tensor("ptrW", [128, 1024], BF16))

    chunk_groups = []
    s0 = 0
    while s0 < NQC:
        n = min(3, NQC - s0)
        chunk_groups.append((s0, n))
        s0 += n
    NG = len(chunk_groups)

    def _phase2(sync):
        ldw = [S("p2_ldw0"), S("p2_ldw1")]
        pe = S("p2_pe")
        dv = S("p2_dv")
        ac = S("p2_ac")
        stq2 = [S("p2_stq0"), S("p2_stq1")]
        cast_v = {0: 0, 1: 0}
        wfree_v = {0: 0, 1: 0}
        pgev_v = {0: None, 1: None}
        stgf_v = {0: 0, 1: 0}

        for tc in range(NQC):
            if tc >= 1:
                nc.tensor.wait_ge(ac.h, ac.v)
            for ic in range(NCH):
                pe.inc(nc.tensor.matmul(
                    pgA[0:80, 0:512],
                    BsT[:, ic * 80:(ic + 1) * 80],
                    xn_fm[:, ic * T + tc * 512: ic * T + (tc + 1) * 512],
                    start=(ic == 0), stop=(ic == NCH - 1)))
            nc.scalar.wait_ge(pe.h, pe.v)
            ac.inc(nc.scalar.activation(xBs[0:80, tc * 512:(tc + 1) * 512],
                                        pgA[0:80, 0:512], AF.Copy,
                                        scale=LORA_SCALING))
        pgev_v[0] = (ac, ac.v)

        projs = [
            ("q", wq_d, s_qT, True, bqs_c, 0),
            ("k", wk_d, s_kT, False, bk_c, 32),
            ("v", wv_d, s_vT, False, bv_c, 64),
        ]
        ldw[0].dinc(sync.dma_start(wbuf[0].ap(), wq_d.ap()[0:128, :]))
        it = 0
        gidx = 0
        for pi, (pname, w_d, dst_d, scaled, bias_c, bandofs) in enumerate(projs):
            for och in range(NCH):
                sl = it % 2
                nxt = None
                if och + 1 < NCH:
                    nxt = (w_d, och + 1)
                elif pi + 1 < len(projs):
                    nxt = (projs[pi + 1][1], 0)
                if nxt is not None:
                    if it >= 1:
                        sync.wait_ge(dv.h, cast_v[(it + 1) % 2])
                    ldw[(it + 1) % 2].dinc(sync.dma_start(
                        wbuf[(it + 1) % 2].ap(),
                        nxt[0].ap()[nxt[1] * 128:(nxt[1] + 1) * 128, :]))

                nc.vector.wait_ge(ldw[sl].h, ldw[sl].v)
                if it >= 2:
                    nc.vector.wait_ge(pe.h, wfree_v[sl])
                cv = dv.inc(nc.vector.tensor_copy(wbf[sl].ap(), wbuf[sl].ap()))
                cast_v[sl] = cv

                nc.tensor.wait_ge(dv.h, cv)
                for half in range(0, NCH, 8):
                    nh = min(8, NCH - half)
                    if half > 0 or it > 0:
                        nc.tensor.wait_ge(dv.h, dv.v)
                    for j in range(nh):
                        ins = nc.tensor.transpose(
                            ptrW[:, j * 128:(j + 1) * 128],
                            wbf[sl][:, (half + j) * 128:(half + j + 1) * 128],
                            ident[:])
                        if j == nh - 1:
                            pe.inc(ins)
                    nc.vector.wait_ge(pe.h, pe.v)
                    dv.inc(nc.vector.tensor_copy(
                        wT[sl][:, half * 128:(half + nh) * 128],
                        ptrW[:, 0:nh * 128]))
                wT_ready = dv.v

                nc.tensor.wait_ge(dv.h, wT_ready)
                for (c0, ncks) in chunk_groups:
                    psl = gidx % 2
                    pg = pgA if psl == 0 else pgB
                    if pgev_v[psl] is not None:
                        psem, pval = pgev_v[psl]
                        nc.tensor.wait_ge(psem.h, pval)
                    for ic in range(NCH):
                        for k in range(ncks):
                            tc = c0 + k
                            pe.inc(nc.tensor.matmul(
                                pg[:, k * 512:(k + 1) * 512],
                                wT[sl][:, ic * 128:(ic + 1) * 128],
                                xn_fm[:, ic * T + tc * 512: ic * T + (tc + 1) * 512],
                                start=(ic == 0), stop=False,
                                skip_group_check=True))
                    for k in range(ncks):
                        tc = c0 + k
                        pe.inc(nc.tensor.matmul(
                            pg[:, k * 512:(k + 1) * 512],
                            ATqkv[bandofs:bandofs + 16, och * 128:(och + 1) * 128],
                            xBs[bandofs:bandofs + 16, tc * 512:(tc + 1) * 512],
                            start=False, stop=True, skip_group_check=True))
                    wfree_v[sl] = pe.v
                    nc.vector.wait_ge(pe.h, pe.v)
                    if stgf_v[psl]:
                        nc.vector.wait_ge(stq2[psl].h, stgf_v[psl])
                    v_ev = dv.inc(nc.vector.tensor_scalar(
                        stg[psl][:, 0:ncks * 512], pg[:, 0:ncks * 512],
                        (SCALE if scaled else 1.0),
                        bias_c.ap()[:, och:och + 1], ALU.mult, ALU.add))
                    pgev_v[psl] = (dv, v_ev)
                    nc.scalar.wait_ge(dv.h, v_ev)
                    stq2[psl].dinc(nc.scalar.dma_start(
                        dst_d.ap()[och * 128:(och + 1) * 128,
                                   c0 * 512:(c0 + ncks) * 512],
                        stg[psl][:, 0:ncks * 512]))
                    stgf_v[psl] = stq2[psl].v
                    gidx += 1
                it += 1

        sync.wait_ge(ldw[0].h, ldw[0].v)
        sync.wait_ge(ldw[1].h, ldw[1].v)
        nc.scalar.wait_ge(stq2[0].h, stq2[0].v)
        nc.scalar.wait_ge(stq2[1].h, stq2[1].v)

    PHASES.append(_phase2)
    es_p2.close()
    es_xn.close()

    es_p3 = ExitStack()
    kT_b = [es_p3.enter_context(nc.sbuf_tensor(f"kT{i}", [128, T], BF16)) for i in range(2)]
    qT_b = [es_p3.enter_context(nc.sbuf_tensor(f"qT{i}", [128, T], BF16)) for i in range(2)]
    vTf_b = [es_p3.enter_context(nc.sbuf_tensor(f"vTf{i}", [128, T], BF16)) for i in range(2)]
    vh2 = [es_p3.enter_context(nc.sbuf_tensor(f"vh{i}", [128, T], BF16)) for i in range(2)]
    pT2 = [es_p3.enter_context(nc.sbuf_tensor(f"pT{i}", [128, NT * 512], BF16)) for i in range(2)]
    sq3b = [es_p3.enter_context(nc.sbuf_tensor(f"sq3{i}", [128, T], BF16)) for i in range(2)]
    mxp = es_p3.enter_context(nc.sbuf_tensor("mxp", [1, NQC], F32))
    mxs = es_p3.enter_context(nc.sbuf_tensor("mxs", [1, 4], F32))
    negc2 = [es_p3.enter_context(nc.sbuf_tensor(f"negc{i}", [128, 1], F32)) for i in range(2)]
    rec2 = [es_p3.enter_context(nc.sbuf_tensor(f"rec2{i}", [1, 512], F32)) for i in range(2)]
    obuf = [es_p3.enter_context(nc.sbuf_tensor(f"obuf{i}", [128, 512], F32)) for i in range(2)]
    ptree2 = [es_p3.enter_context(nc.sbuf_tensor(f"ptree{i}", [128, NT * 256], BF16)) for i in range(2)]
    ostg = [es_p3.enter_context(nc.sbuf_tensor(f"ostg{i}", [128, 512], BF16)) for i in range(2)]
    sg3 = [es_p3.enter_context(nc.psum_tensor(f"sg3{i}", [128, 1024], F32)) for i in range(3)]
    avp = [es_p3.enter_context(nc.psum_tensor(f"avp{i}", [128, 512], F32)) for i in range(2)]

    def _phase3(sync):
        ldk = [S("p3_ldk0"), S("p3_ldk1")]
        ldq = [S("p3_ldq0"), S("p3_ldq1")]
        ldv = [S("p3_ldv0"), S("p3_ldv1")]
        pe = S("p3_pe")
        dv = S("p3_dv")
        ac = S("p3_ac")
        sto3 = [S("p3_sto0"), S("p3_sto1")]
        av_done = 0
        av_done_slot = {0: 0, 1: 0}
        ost_done = {0: 0, 1: 0}
        avfree = {0: 0, 1: 0}
        rot = {"i": 0}
        bank_waits = {0: [], 1: [], 2: []}
        sq3_free = {0: 0, 1: 0}
        vh_slot_free = {0: 0, 1: 0}
        tree_a = [0]
        negc_ready = {0: 0, 1: 0}


        def take_bank():
            b = rot["i"] % 3
            rot["i"] += 1
            for (sem, val) in bank_waits[b]:
                nc.tensor.wait_ge(sem.h, val)
            bank_waits[b] = []
            return b

        pending_sums = None
        pending_fin = None

        def emit_fin_stage(pend, cur_qsl):
            ph, pq0, pqsl, pv_ob = pend
            nc.tensor.wait_ge(dv.h, pv_ob)
            nc.tensor.wait_ge(dv.h, rec_ready[pqsl])
            pv = pe.inc(nc.tensor.matmul(avp[cur_qsl][:], ones_f32r.ap(),
                                         rec2[pqsl][0:1, :], start=True, stop=True))
            rec_read[pqsl] = pv
            nc.vector.wait_ge(pe.h, pv)
            nc.vector.wait_ge(dv.h, pv_ob)
            if ost_done[pqsl]:
                nc.vector.wait_ge(sto3[pqsl].h, ost_done[pqsl])
            v_os = dv.inc(nc.vector.tensor_tensor(ostg[pqsl].ap(), obuf[pqsl].ap(),
                                                  avp[cur_qsl][:], ALU.mult))
            avfree[cur_qsl] = v_os
            nc.scalar.wait_ge(dv.h, v_os)
            sto3[pqsl].dinc(nc.scalar.dma_start(
                s_o.ap()[ph * 128:(ph + 1) * 128, pq0:pq0 + 512],
                ostg[pqsl].ap()))
            ost_done[pqsl] = sto3[pqsl].v

        def emit_sums_stage(pend, cur_qsl):
            pqsl, pv_part, ph_s, pq0_s = pend
            nc.vector.wait_ge(pe.h, av_done)
            v_ob = dv.inc(nc.vector.tensor_copy(obuf[pqsl].ap(), avp[pqsl][:]))
            nc.tensor.wait_ge(dv.h, pv_part)
            nc.tensor.wait_ge(dv.h, v_ob)
            ptree_mm[pqsl] = pe.inc(nc.tensor.matmul(
                avp[pqsl][0:1, 0:512], ones_bfc.ap(), ptree2[pqsl][:, 0:512],
                start=True, stop=True, skip_group_check=True))
            if rec_read[pqsl]:
                nc.vector.wait_ge(pe.h, rec_read[pqsl])
            nc.vector.wait_ge(pe.h, ptree_mm[pqsl])
            v_rec = dv.inc(nc.vector.reciprocal(rec2[pqsl][0:1, :],
                                                avp[pqsl][0:1, 0:512]))
            rec_ready[pqsl] = v_rec
            avfree[pqsl] = v_rec
            return (ph_s, pq0_s, pqsl, v_ob)

        ptree_mm = {0: 0, 1: 0}
        rec_ready = {0: 0, 1: 0}
        rec_read = {0: 0, 1: 0}
        vpart_slot = {0: 0, 1: 0}
        sumbank = {}

        def emit_vtrans(hh):
            psl = hh % 2
            nc.tensor.wait_ge(ldv[psl].h, ldv[psl].v)
            if vh_slot_free[psl]:
                nc.tensor.wait_ge(pe.h, vh_slot_free[psl])
            for kt0 in range(0, NT, 16):
                nb = min(16, NT - kt0)
                b = take_bank()
                bview = sg3[b][:].bitcast(BF16)
                for j in range(nb):
                    kt = kt0 + j
                    ins = nc.tensor.transpose(bview[:, j * 128:(j + 1) * 128],
                                              vTf_b[psl][:, kt * 128:(kt + 1) * 128],
                                              ident[:])
                    if j == nb - 1:
                        pe.inc(ins)
                nc.vector.wait_ge(pe.h, pe.v)
                ev = dv.inc(nc.vector.tensor_copy(vh2[psl][:, kt0 * 128:(kt0 + nb) * 128],
                                                  bview[:, 0:nb * 128]))
                bank_waits[b].append((dv, ev))

        def emit_norm(hh):
            psl = hh % 2
            nc.scalar.wait_ge(ldk[psl].h, ldk[psl].v)
            nc.scalar.wait_ge(ldq[psl].h, ldq[psl].v)
            if sq3_free[psl]:
                nc.scalar.wait_ge(pe.h, sq3_free[psl])
            for (srcb, col) in ((kT_b[psl], 1), (qT_b[psl], 0)):
                if col == 0:
                    nc.scalar.wait_ge(pe.h, sq3_free[psl])
                aq = ac.inc(nc.scalar.activation(sq3b[psl].ap(), srcb.ap(), AF.Square))
                nc.tensor.wait_ge(ac.h, aq)
                nc.vector.wait_ge(dv.h, dv.v)
                for tc in range(NQC):
                    b = take_bank()
                    pe.inc(nc.tensor.matmul(
                        sg3[b][0:1, 0:512], ones_bfc.ap(),
                        sq3b[psl][:, tc * 512:(tc + 1) * 512], start=True, stop=True))
                    nc.vector.wait_ge(pe.h, pe.v)
                    rv = dv.inc(nc.vector.reduce_max(mxp[0:1, tc:tc + 1],
                                                     sg3[b][0:1, 0:512], axis=AX.X))
                    bank_waits[b].append((dv, rv))
                sq3_free[psl] = pe.v
                nc.vector.wait_ge(dv.h, dv.v)
                dv.inc(nc.vector.reduce_max(mxs[0:1, col:col + 1], mxp[0:1, 0:NQC],
                                            axis=AX.X))
            nc.vector.wait_ge(dv.h, dv.v)
            v_m2 = dv.inc(nc.vector.tensor_tensor(mxs[0:1, 2:3], mxs[0:1, 0:1],
                                                  mxs[0:1, 1:2], ALU.mult))
            nc.scalar.wait_ge(dv.h, v_m2)
            a_l = ac.inc(nc.scalar.activation(mxs[0:1, 3:4], mxs[0:1, 2:3], AF.Ln,
                                              scale=1.002))
            nc.scalar.wait_ge(ac.h, a_l)
            a_c = ac.inc(nc.scalar.activation(mxs[0:1, 3:4], mxs[0:1, 3:4], AF.Exp,
                                              scale=LORA_SCALING))
            nc.scalar.wait_ge(ac.h, a_c)
            a_cn = ac.inc(nc.scalar.mul(mxs[0:1, 3:4], mxs[0:1, 3:4], -1.0))
            b = take_bank()
            nc.tensor.wait_ge(ac.h, a_cn)
            pv = pe.inc(nc.tensor.matmul(sg3[b][:, 0:1], ones_f32r.ap(),
                                         mxs[0:1, 3:4], start=True, stop=True))
            nc.vector.wait_ge(pe.h, pv)
            v_nc = dv.inc(nc.vector.tensor_copy(negc2[psl].ap(), sg3[b][:, 0:1]))
            bank_waits[b].append((dv, v_nc))
            negc_ready[psl] = v_nc

        ldk[0].dinc(sync.dma_start(kT_b[0].ap(), s_kT.ap()[0:128, :]))
        ldq[0].dinc(sync.dma_start(qT_b[0].ap(), s_qT.ap()[0:128, :]))
        ldv[0].dinc(sync.dma_start(vTf_b[0].ap(), s_vT.ap()[0:128, :]))
        emit_vtrans(0)
        emit_norm(0)

        for h in range(H):
            sl = h % 2
            if h + 1 < H:
                if h >= 1:
                    sync.wait_ge(pe.h, av_done)
                r0 = (h + 1) * 128
                ldk[(h + 1) % 2].dinc(sync.dma_start(kT_b[(h + 1) % 2].ap(),
                                                     s_kT.ap()[r0:r0 + 128, :]))
                ldq[(h + 1) % 2].dinc(sync.dma_start(qT_b[(h + 1) % 2].ap(),
                                                     s_qT.ap()[r0:r0 + 128, :]))
                ldv[(h + 1) % 2].dinc(sync.dma_start(vTf_b[(h + 1) % 2].ap(),
                                                     s_vT.ap()[r0:r0 + 128, :]))

            if h >= 1:
                emit_vtrans(h)
                emit_norm(h)

            s_done_cur = []
            s_bank_cur = []
            for qc in range(NQC):
                qsl = qc % 2
                q0 = qc * 512

                if pending_sums is not None:
                    fin2 = emit_sums_stage(pending_sums, qsl)
                    pending_sums = None
                else:
                    fin2 = None
                if pending_fin is not None:
                    assert pending_fin[2] == qsl, (h, qc, pending_fin)
                    emit_fin_stage(pending_fin, qsl)
                pending_fin = fin2

                def emit_S(g, dstv, dstb, qq0):
                    b = take_bank()
                    for j in range(2):
                        kt = 2 * g + j
                        ins = nc.tensor.matmul(
                            sg3[b][:, j * 512:(j + 1) * 512],
                            kT_b[sl][:, kt * 128:(kt + 1) * 128],
                            qT_b[sl][:, qq0:qq0 + 512],
                            start=True, stop=True, skip_group_check=True)
                        if j == 1:
                            pe.inc(ins)
                    dstv.append(pe.v)
                    dstb.append(b)

                if qc == 0:
                    s_done_cur, s_bank_cur = [], []
                    for g0 in range(min(3, NKG)):
                        emit_S(g0, s_done_cur, s_bank_cur, q0)
                s_done, s_bank = s_done_cur, s_bank_cur
                for g in range(NKG):
                    nc.scalar.wait_ge(pe.h, s_done[g])
                    if g == 0:
                        nc.scalar.wait_ge(pe.h, av_done_slot[qsl])
                        if vpart_slot[qsl]:
                            nc.scalar.wait_ge(dv.h, vpart_slot[qsl])
                        if qc == 0:
                            nc.scalar.wait_ge(dv.h, negc_ready[sl])
                    e = ac.inc(nc.scalar.activation(
                        pT2[qsl][:, g * 1024:(g + 1) * 1024], sg3[s_bank[g]][:],
                        AF.Exp, bias=negc2[sl].ap()))
                    bank_waits[s_bank[g]].append((ac, e))
                    nc.tensor.wait_ge(ac.h, e)
                    if g == 0:
                        if avfree[qsl]:
                            nc.tensor.wait_ge(dv.h, avfree[qsl])
                    if g + 3 < NKG:
                        emit_S(g + 3, s_done, s_bank, q0)
                    for j in range(2):
                        kt = 2 * g + j
                        pe.inc(nc.tensor.matmul(
                            avp[qsl][:], vh2[sl][:, kt * 128:(kt + 1) * 128],
                            pT2[qsl][:, kt * 512:(kt + 1) * 512],
                            start=(kt == 0), stop=(kt == NT - 1),
                            skip_group_check=True))
                    if g == (3 * NT // 4 - 1) // 2:
                        wq = NT * 256
                        if ptree_mm[qsl]:
                            nc.vector.wait_ge(pe.h, ptree_mm[qsl])
                        nc.vector.wait_ge(ac.h, e)
                        with nc.allow_low_precision(reason="bf16 tree partials (softmax sums; final 128-way sum in fp32 PSUM)"):
                            tree_a[0] = dv.inc(nc.vector.tensor_tensor(
                                ptree2[qsl][:, 0:wq // 2], pT2[qsl][:, 0:wq // 2],
                                pT2[qsl][:, wq:wq + wq // 2], ALU.add))
                if qc + 1 < NQC:
                    s_done_cur, s_bank_cur = [], []
                    for g0 in range(min(3, NKG)):
                        emit_S(g0, s_done_cur, s_bank_cur, (qc + 1) * 512)
                av_done = pe.v
                av_done_slot[qsl] = pe.v

                nc.vector.wait_ge(ac.h, ac.v)
                w = NT * 256
                with nc.allow_low_precision(reason="bf16 tree partials (softmax sums; final 128-way sum in fp32 PSUM)"):
                    v_part = dv.inc(nc.vector.tensor_tensor(
                        ptree2[qsl][:, w // 2:w], pT2[qsl][:, w // 2:w],
                        pT2[qsl][:, w + w // 2:2 * w], ALU.add))
                    while w > 512:
                        nc.vector.wait_ge(dv.h, v_part)
                        if tree_a[0]:
                            nc.vector.wait_ge(dv.h, tree_a[0])
                            tree_a[0] = 0
                        v_part = dv.inc(nc.vector.tensor_tensor(
                            ptree2[qsl][:, 0:w // 2], ptree2[qsl][:, 0:w // 2],
                            ptree2[qsl][:, w // 2:w], ALU.add))
                        w //= 2
                vpart_slot[qsl] = v_part
                pending_sums = (qsl, v_part, h, q0)
            vh_slot_free[sl] = av_done

        if pending_sums is not None:
            fin2 = emit_sums_stage(pending_sums, 1 - pending_sums[0])
        else:
            fin2 = None
        if pending_fin is not None:
            emit_fin_stage(pending_fin, pending_fin[2])
        if fin2 is not None:
            emit_fin_stage(fin2, fin2[2])

        sync.wait_ge(ldk[0].h, ldk[0].v)
        sync.wait_ge(ldk[1].h, ldk[1].v)
        sync.wait_ge(ldq[0].h, ldq[0].v)
        sync.wait_ge(ldq[1].h, ldq[1].v)
        sync.wait_ge(ldv[0].h, ldv[0].v)
        sync.wait_ge(ldv[1].h, ldv[1].v)
        nc.scalar.wait_ge(sto3[0].h, sto3[0].v)
        nc.scalar.wait_ge(sto3[1].h, sto3[1].v)

    PHASES.append(_phase3)
    es_p3.close()

    es_p4a = ExitStack()
    woT = es_p4a.enter_context(nc.sbuf_tensor("woT", [128, NCH * D], BF16))
    wbuf4 = [es_p4a.enter_context(nc.sbuf_tensor(f"wbuf4{i}", [128, D], F32)) for i in range(2)]
    wbf4 = es_p4a.enter_context(nc.sbuf_tensor("wbf4", [128, D], BF16))
    o_sp = [es_p4a.enter_context(nc.sbuf_tensor(f"o_sp{i}", [128, NCH * 512], BF16)) for i in range(2)]
    x_sp = [es_p4a.enter_context(nc.sbuf_tensor(f"x_sp{i}", [128, D], F32)) for i in range(2)]
    res4 = [es_p4a.enter_context(nc.sbuf_tensor(f"res4{i}", [128, 1024], F32)) for i in range(2)]
    oBs4 = es_p4a.enter_context(nc.sbuf_tensor("oBs4", [16, 512], BF16))
    bo_row = es_p4a.enter_context(nc.sbuf_tensor("bo_row", [1, D], BF16))
    bo_row32 = es_p4a.enter_context(nc.sbuf_tensor("bo_row32", [1, D], F32))
    pa = [es_p4a.enter_context(nc.psum_tensor(f"pa{i}", [128, 1024], F32)) for i in range(2)]
    pob = es_p4a.enter_context(nc.psum_tensor("pob", [128, 512], F32))
    ptr4 = es_p4a.enter_context(nc.psum_tensor("ptr4", [128, 1024], BF16))

    NSP = T // 512
    woT_v = woT.ap().rearrange("p (c n) -> p c n", n=D)
    so_v = s_o.ap().rearrange("(c p) t -> p c t", p=128)

    def _phase4(sync):
        ldw = [S("p4a_ldw0"), S("p4a_ldw1")]
        ldo = [S("p4a_ldo0"), S("p4a_ldo1")]
        ldx4 = [S("p4a_ldx0"), S("p4a_ldx1")]
        pe = S("p4a_pe")
        dv = S("p4a_dv")
        ac = S("p4a_ac")
        sto4 = [S("p4a_sto0"), S("p4a_sto1")]
        resfree = {0: 0, 1: 0}
        res_spill = {0: 0, 1: 0}
        cast_v = 0
        obev = 0
        osp_done = 0
        x_used = 0

        ldb = S("p4a_ldb")
        ldb.dinc(sync.dma_start(bo_row32.ap(), bo_d.ap()))
        nc.vector.wait_ge(ldb.h, ldb.v)
        dv.inc(nc.vector.tensor_copy(bo_row.ap(), bo_row32.ap()))

        ldw[0].dinc(sync.dma_start(wbuf4[0].ap(), wo_d.ap()[0:128, :]))
        for oc in range(NCH):
            sl = oc % 2
            if oc + 1 < NCH:
                if oc >= 1:
                    sync.wait_ge(dv.h, cast_v)
                ldw[(oc + 1) % 2].dinc(sync.dma_start(
                    wbuf4[(oc + 1) % 2].ap(), wo_d.ap()[(oc + 1) * 128:(oc + 2) * 128, :]))
            nc.vector.wait_ge(ldw[sl].h, ldw[sl].v)
            nc.vector.wait_ge(pe.h, pe.v)
            cast_v = dv.inc(nc.vector.tensor_copy(wbf4.ap(), wbuf4[sl].ap()))
            nc.tensor.wait_ge(dv.h, cast_v)
            for half in range(0, NCH, 8):
                nh = min(8, NCH - half)
                if half > 0 or oc > 0:
                    nc.tensor.wait_ge(dv.h, dv.v)
                for j in range(nh):
                    ins = nc.tensor.transpose(
                        ptr4[:, j * 128:(j + 1) * 128],
                        wbf4[:, (half + j) * 128:(half + j + 1) * 128], ident[:])
                    if j == nh - 1:
                        pe.inc(ins)
                nc.vector.wait_ge(pe.h, pe.v)
                dv.inc(nc.vector.tensor_copy(
                    woT_v[:, half:half + nh, oc * 128:(oc + 1) * 128],
                    ptr4[:, 0:nh * 128].rearrange("p (c n) -> p c n", n=128)))

        nc.tensor.wait_ge(dv.h, dv.v)
        ldo[0].dinc(sync.dma_start(
            o_sp[0].ap().rearrange("p (c n) -> p c n", n=512), so_v[:, :, 0:512]))
        ldx4[0].dinc(sync.dma_start(x_sp[0].ap(), x_d.ap()[0:128, :]))

        for sp in range(NSP):
            sl = sp % 2
            if sp + 1 < NSP:
                if sp >= 1:
                    sync.wait_ge(pe.h, osp_done)
                ldo[(sp + 1) % 2].dinc(sync.dma_start(
                    o_sp[(sp + 1) % 2].ap().rearrange("p (c n) -> p c n", n=512),
                    so_v[:, :, (sp + 1) * 512:(sp + 2) * 512]))

            nc.tensor.wait_ge(ldo[sl].h, ldo[sl].v)
            if obev:
                nc.tensor.wait_ge(ac.h, obev)
            for ic in range(NCH):
                pe.inc(nc.tensor.matmul(
                    pob[0:16, :], BoT[:, ic * 16:(ic + 1) * 16],
                    o_sp[sl][:, ic * 512:(ic + 1) * 512],
                    start=(ic == 0), stop=(ic == NCH - 1)))
            nc.scalar.wait_ge(pe.h, pe.v)
            obev = ac.inc(nc.scalar.activation(oBs4.ap(), pob[0:16, :], AF.Copy,
                                               scale=LORA_SCALING))

            for tt in range(4):
                t = sp * 4 + tt
                if t + 1 < NT:
                    if t >= 1:
                        sync.wait_ge(dv.h, x_used)
                    ldx4[(t + 1) % 2].dinc(sync.dma_start(
                        x_sp[(t + 1) % 2].ap(), x_d.ap()[(t + 1) * 128:(t + 2) * 128, :]))
                for hf in range(NHF):
                    psl = (t * NHF + hf) % 2
                    if resfree[psl]:
                        nc.tensor.wait_ge(dv.h, resfree[psl])
                    nc.tensor.wait_ge(ac.h, obev)
                    for ocs in range(2):
                        ocol = hf * 1024 + ocs * 512
                        for ic in range(NCH):
                            pe.inc(nc.tensor.matmul(
                                pa[psl][:, ocs * 512:(ocs + 1) * 512],
                                o_sp[sl][:, ic * 512 + tt * 128: ic * 512 + tt * 128 + 128],
                                woT[:, ic * D + ocol: ic * D + ocol + 512],
                                start=(ic == 0), stop=False, skip_group_check=True))
                        pe.inc(nc.tensor.matmul(
                            pa[psl][:, ocs * 512:(ocs + 1) * 512],
                            oBs4[0:16, tt * 128:(tt + 1) * 128],
                            ATo[0:16, ocol:ocol + 512],
                            start=False, stop=False, skip_group_check=True))
                        pe.inc(nc.tensor.matmul(
                            pa[psl][:, ocs * 512:(ocs + 1) * 512],
                            ones_bfr.ap(), bo_row[0:1, ocol:ocol + 512],
                            start=False, stop=True, skip_group_check=True))
                    nc.vector.wait_ge(pe.h, pe.v)
                    nc.vector.wait_ge(ldx4[t % 2].h, ldx4[t % 2].v)
                    if res_spill[psl]:
                        nc.vector.wait_ge(sto4[psl].h, res_spill[psl])
                    rv = dv.inc(nc.vector.tensor_tensor(
                        res4[psl].ap(), pa[psl][:],
                        x_sp[t % 2][:, hf * 1024:(hf + 1) * 1024], ALU.add))
                    resfree[psl] = rv
                    if hf == NHF - 1:
                        x_used = rv
                    nc.scalar.wait_ge(dv.h, rv)
                    sto4[psl].dinc(nc.scalar.dma_start(
                        out_d.ap()[t * 128:(t + 1) * 128, hf * 1024:(hf + 1) * 1024],
                        res4[psl].ap()))
                    res_spill[psl] = sto4[psl].v
            osp_done = pe.v

        sync.wait_ge(ldo[0].h, ldo[0].v)
        sync.wait_ge(ldo[1].h, ldo[1].v)
        sync.wait_ge(ldx4[0].h, ldx4[0].v)
        sync.wait_ge(ldx4[1].h, ldx4[1].v)
        sync.wait_ge(ldw[0].h, ldw[0].v)
        sync.wait_ge(ldw[1].h, ldw[1].v)
        nc.scalar.wait_ge(sto4[0].h, sto4[0].v)
        nc.scalar.wait_ge(sto4[1].h, sto4[1].v)

    PHASES.append(_phase4)
    es_p4a.close()

    es_p4b = ExitStack()
    swT = es_p4b.enter_context(nc.sbuf_tensor("swT", [128, NCH * D], BF16))
    fwb4 = [es_p4b.enter_context(nc.sbuf_tensor(f"fwb4{i}", [128, D], F32)) for i in range(2)]
    swsg = es_p4b.enter_context(nc.sbuf_tensor("swsg", [128, D], BF16))
    a_sp = [es_p4b.enter_context(nc.sbuf_tensor(f"a_sp{i}", [128, NCH * 512], BF16)) for i in range(2)]
    ord_b = [es_p4b.enter_context(nc.sbuf_tensor(f"ord{i}", [128, D], F32)) for i in range(2)]
    tf4 = [es_p4b.enter_context(nc.sbuf_tensor(f"tf4{i}", [128, 1024], F32)) for i in range(2)]
    res4b = [es_p4b.enter_context(nc.sbuf_tensor(f"res4b{i}", [128, 1024], F32)) for i in range(2)]
    al_bc = es_p4b.enter_context(nc.sbuf_tensor("al_bc", [128, D], F32))
    al_row = es_p4b.enter_context(nc.sbuf_tensor("al_row", [1, D], F32))
    fb_row = es_p4b.enter_context(nc.sbuf_tensor("fb_row", [1, D], BF16))
    fb_row32 = es_p4b.enter_context(nc.sbuf_tensor("fb_row32", [1, D], F32))
    rs4 = es_p4b.enter_context(nc.sbuf_tensor("rs4", [128, 4], F32))
    sw_scr = es_p4b.enter_context(nc.sbuf_tensor("sw_scr", [128, D], F32))
    pf = [es_p4b.enter_context(nc.psum_tensor(f"pf{i}", [128, 1024], F32)) for i in range(2)]
    ptr4b = es_p4b.enter_context(nc.psum_tensor("ptr4b", [128, 1024], BF16))
    pbc4 = es_p4b.enter_context(nc.psum_tensor("pbc4", [128, 512], F32))

    swT_v = swT.ap().rearrange("p (c n) -> p c n", n=D)
    sa_v = s_a.ap().rearrange("(c p) t -> p c t", p=128)

    def _phase5(sync):
        ldw = [S("p4b_ldw0"), S("p4b_ldw1")]
        lda = [S("p4b_lda0"), S("p4b_lda1")]
        ldr = [S("p4b_ldr0"), S("p4b_ldr1")]
        pe = S("p4b_pe")
        dv = S("p4b_dv")
        ac = S("p4b_ac")
        sto5 = [S("p4b_sto0"), S("p4b_sto1")]
        resfree = {0: 0, 1: 0}
        res_spill = {0: 0, 1: 0}
        sg_v = 0
        last_res_v = 0
        asp_done = 0
        o_used = 0

        ldb2 = S("p4b_ldb")
        ldb2.dinc(sync.dma_start(al_row.ap(), al_d.ap()))
        ldb2.dinc(sync.dma_start(fb_row32.ap(), fb_d.ap()))
        nc.vector.wait_ge(ldb2.h, ldb2.v)
        dv.inc(nc.vector.tensor_copy(fb_row.ap(), fb_row32.ap()))
        nc.tensor.wait_ge(ldb2.h, ldb2.v)
        for j in range(D // 512):
            if j > 0:
                nc.tensor.wait_ge(dv.h, dv.v)
            pe.inc(nc.tensor.matmul(pbc4[:], ones_f32r.ap(),
                                    al_row[0:1, j * 512:(j + 1) * 512],
                                    start=True, stop=True))
            nc.vector.wait_ge(pe.h, pe.v)
            dv.inc(nc.vector.tensor_copy(al_bc[:, j * 512:(j + 1) * 512], pbc4[:]))

        ldw[0].dinc(sync.dma_start(fwb4[0].ap(), fw_d.ap()[0:128, :]))
        for oc in range(NCH):
            sl = oc % 2
            if oc + 1 < NCH:
                if oc >= 1:
                    sync.wait_ge(ac.h, sg_v)
                    sync.wait_ge(dv.h, last_res_v)
                ldw[(oc + 1) % 2].dinc(sync.dma_start(
                    fwb4[(oc + 1) % 2].ap(), fw_d.ap()[(oc + 1) * 128:(oc + 2) * 128, :]))
            nc.vector.wait_ge(ldw[sl].h, ldw[sl].v)
            v0 = dv.inc(nc.vector.reduce_sum(rs4[:, 0:1], fwb4[sl].ap(), axis=AX.X))
            nc.vector.wait_ge(dv.h, v0)
            v1 = dv.inc(nc.vector.tensor_scalar_mul(rs4[:, 1:2], rs4[:, 0:1], INV_D))
            nc.vector.wait_ge(dv.h, v1)
            v2 = dv.inc(nc.vector.tensor_scalar(
                sw_scr.ap(), fwb4[sl].ap(), rs4[:, 1:2], None, ALU.subtract))
            nc.vector.wait_ge(dv.h, v2)
            v3 = dv.inc(nc.vector.reduce_sum(rs4[:, 2:3], sw_scr.ap(), axis=AX.X))
            nc.vector.wait_ge(dv.h, v3)
            v4 = dv.inc(nc.vector.tensor_scalar(
                rs4[:, 3:4], rs4[:, 2:3], NEG_INV_D, rs4[:, 1:2],
                ALU.mult, ALU.subtract))
            last_res_v = v2
            nc.scalar.wait_ge(dv.h, v4)
            nc.scalar.wait_ge(pe.h, pe.v)
            sg_v = ac.inc(nc.scalar.activation(swsg.ap(), fwb4[sl].ap(), AF.Sign,
                                               bias=rs4[:, 3:4]))
            nc.tensor.wait_ge(ac.h, sg_v)
            for half in range(0, NCH, 8):
                nh = min(8, NCH - half)
                if half > 0 or oc > 0:
                    nc.tensor.wait_ge(dv.h, dv.v)
                for j in range(nh):
                    ins = nc.tensor.transpose(
                        ptr4b[:, j * 128:(j + 1) * 128],
                        swsg[:, (half + j) * 128:(half + j + 1) * 128], ident[:])
                    if j == nh - 1:
                        pe.inc(ins)
                nc.vector.wait_ge(pe.h, pe.v)
                dv.inc(nc.vector.tensor_copy(
                    swT_v[:, half:half + nh, oc * 128:(oc + 1) * 128],
                    ptr4b[:, 0:nh * 128].rearrange("p (c n) -> p c n", n=128)))

        nc.tensor.wait_ge(dv.h, dv.v)
        lda[0].dinc(sync.dma_start(
            a_sp[0].ap().rearrange("p (c n) -> p c n", n=512), sa_v[:, :, 0:512]))
        ldr[0].dinc(sync.dma_start(ord_b[0].ap(), out_d.ap()[0:128, :]))

        NSP2 = T // 512
        for sp in range(NSP2):
            sl = sp % 2
            if sp + 1 < NSP2:
                if sp >= 1:
                    sync.wait_ge(pe.h, asp_done)
                lda[(sp + 1) % 2].dinc(sync.dma_start(
                    a_sp[(sp + 1) % 2].ap().rearrange("p (c n) -> p c n", n=512),
                    sa_v[:, :, (sp + 1) * 512:(sp + 2) * 512]))
            nc.tensor.wait_ge(lda[sl].h, lda[sl].v)
            for tt in range(4):
                t = sp * 4 + tt
                if t + 1 < NT:
                    if t >= 1:
                        sync.wait_ge(dv.h, o_used)
                    ldr[(t + 1) % 2].dinc(sync.dma_start(
                        ord_b[(t + 1) % 2].ap(), out_d.ap()[(t + 1) * 128:(t + 2) * 128, :]))
                for hf in range(NHF):
                    psl = (t * NHF + hf) % 2
                    if resfree[psl]:
                        nc.tensor.wait_ge(dv.h, resfree[psl])
                    for ocs in range(2):
                        ocol = hf * 1024 + ocs * 512
                        for ic in range(NCH):
                            pe.inc(nc.tensor.matmul(
                                pf[psl][:, ocs * 512:(ocs + 1) * 512],
                                a_sp[sl][:, ic * 512 + tt * 128: ic * 512 + tt * 128 + 128],
                                swT[:, ic * D + ocol: ic * D + ocol + 512],
                                start=(ic == 0), stop=False, skip_group_check=True))
                        pe.inc(nc.tensor.matmul(
                            pf[psl][:, ocs * 512:(ocs + 1) * 512],
                            ones_bfr.ap(), fb_row[0:1, ocol:ocol + 512],
                            start=False, stop=True, skip_group_check=True))
                    nc.vector.wait_ge(pe.h, pe.v)
                    nc.vector.wait_ge(ldr[t % 2].h, ldr[t % 2].v)
                    v_tf = dv.inc(nc.vector.tensor_tensor(
                        tf4[psl].ap(), pf[psl][:],
                        al_bc[:, hf * 1024:(hf + 1) * 1024], ALU.mult))
                    resfree[psl] = v_tf
                    nc.vector.wait_ge(dv.h, v_tf)
                    if res_spill[psl]:
                        nc.vector.wait_ge(sto5[psl].h, res_spill[psl])
                    rv2 = dv.inc(nc.vector.tensor_tensor(
                        res4b[psl].ap(), tf4[psl].ap(),
                        ord_b[t % 2][:, hf * 1024:(hf + 1) * 1024], ALU.add))
                    if hf == NHF - 1:
                        o_used = rv2
                    nc.scalar.wait_ge(dv.h, rv2)
                    sto5[psl].dinc(nc.scalar.dma_start(
                        out_d.ap()[t * 128:(t + 1) * 128, hf * 1024:(hf + 1) * 1024],
                        res4b[psl].ap()))
                    res_spill[psl] = sto5[psl].v
            asp_done = pe.v

        sync.wait_ge(lda[0].h, lda[0].v)
        sync.wait_ge(lda[1].h, lda[1].v)
        sync.wait_ge(ldr[0].h, ldr[0].v)
        sync.wait_ge(ldr[1].h, ldr[1].v)
        nc.scalar.wait_ge(sto5[0].h, sto5[0].v)
        nc.scalar.wait_ge(sto5[1].h, sto5[1].v)
        sync.wait_ge(sto5[0].h, sto5[0].v)
        sync.wait_ge(sto5[1].h, sto5[1].v)

    PHASES.append(_phase5)
    es_p4b.close()

    def _join():
        engines = [nc.tensor, nc.vector, nc.scalar, nc.gpsimd, nc.sync]
        for eng in engines:
            for sm in sems.values():
                if sm.v:
                    eng.wait_ge(sm.h, sm.v)

    with nc.Block() as block:
        @block.sync
        def _(sync):
            for pi, f in enumerate(PHASES):
                f(sync)
                if pi + 1 < len(PHASES):
                    _join()

    nc.finalize()
    return nc


def make_in_map(b, x, inputs):
    def row(v):
        return np.asarray(v, dtype=np.float32).reshape(1, -1)
    return {
        "x": x[b],
        "wq": inputs["L0_attn_Wq"], "wk": inputs["L0_attn_Wk"],
        "wv": inputs["L0_attn_Wv"], "wo": inputs["L0_attn_Wo"],
        "bq": inputs["L0_attn_bq"], "bk": inputs["L0_attn_bk"],
        "bv": inputs["L0_attn_bv"], "bo": row(inputs["L0_attn_bo"]),
        "Blq": inputs["L0_attn_Bq"], "Blk": inputs["L0_attn_Bk"],
        "Blv": inputs["L0_attn_Bv"], "Blo": inputs["L0_attn_Bo"],
        "Alq": inputs["L0_attn_Aq"], "Alk": inputs["L0_attn_Ak"],
        "Alv": inputs["L0_attn_Av"], "Alo": inputs["L0_attn_Ao"],
        "gm": row(inputs["L0_ffn_gamma"]), "bt": row(inputs["L0_ffn_beta"]),
        "fw": inputs["L0_ffn_weight"], "fb": row(inputs["L0_ffn_bias"]),
        "al": row(inputs["L0_ffn_alpha"]),
    }


def kernel(**inputs):
    x = np.asarray(inputs["L0_attn_x"], dtype=np.float32)
    B, T, D = x.shape
    nc = build_nc(T, D)
    in_maps = [make_in_map(b, x, inputs) for b in range(B)]
    res = run_bass_kernel_spmd(nc, in_maps, core_ids=list(range(B)))
    out = np.stack([np.asarray(res.results[b]["out"]) for b in range(B)], axis=0)
    return out.astype(np.float32)


# version 23
# speedup vs baseline: 1.0602x; 1.0200x over previous
import numpy as np
import ml_dtypes
from contextlib import ExitStack

import concourse.bass as bass
import concourse.mybir as mybir
from concourse.bass_utils import run_bass_kernel_spmd

F32 = mybir.dt.float32
BF16 = mybir.dt.bfloat16
AF = mybir.ActivationFunctionType
ALU = mybir.AluOpType
AX = mybir.AxisListType

EPS = 1e-5
R = 16
LORA_SCALING = 0.5


class Sem:

    def __init__(self, nc, name):
        self.h = nc.alloc_semaphore(name)
        self.v = 0

    def inc(self, instr, k=1):
        instr.then_inc(self.h, k)
        self.v += k
        return self.v

    def dinc(self, instr):
        return self.inc(instr, 16)


def build_nc(T, D):
    HD = 128
    H = D // HD
    NCH = D // 128
    NT = T // 128
    NQC = T // 512
    NKG = NT // 2
    SCALE = float(HD) ** -0.5
    INV_D = 1.0 / D
    NEG_INV_D = -1.0 / D
    NHF = D // 1024

    assert T % 512 == 0 and NT % 2 == 0 and D % 1024 == 0

    nc = bass.Bass("TRN2", target_bir_lowering=False)

    x_d = nc.dram_tensor("x", [T, D], F32, kind="ExternalInput")
    wq_d = nc.dram_tensor("wq", [D, D], F32, kind="ExternalInput")
    wk_d = nc.dram_tensor("wk", [D, D], F32, kind="ExternalInput")
    wv_d = nc.dram_tensor("wv", [D, D], F32, kind="ExternalInput")
    wo_d = nc.dram_tensor("wo", [D, D], F32, kind="ExternalInput")
    bq_d = nc.dram_tensor("bq", [D], F32, kind="ExternalInput")
    bk_d = nc.dram_tensor("bk", [D], F32, kind="ExternalInput")
    bv_d = nc.dram_tensor("bv", [D], F32, kind="ExternalInput")
    bo_d = nc.dram_tensor("bo", [1, D], F32, kind="ExternalInput")
    Bq_d = nc.dram_tensor("Blq", [R, D], F32, kind="ExternalInput")
    Bk_d = nc.dram_tensor("Blk", [R, D], F32, kind="ExternalInput")
    Bv_d = nc.dram_tensor("Blv", [R, D], F32, kind="ExternalInput")
    Bo_d = nc.dram_tensor("Blo", [R, D], F32, kind="ExternalInput")
    Aq_d = nc.dram_tensor("Alq", [D, R], F32, kind="ExternalInput")
    Ak_d = nc.dram_tensor("Alk", [D, R], F32, kind="ExternalInput")
    Av_d = nc.dram_tensor("Alv", [D, R], F32, kind="ExternalInput")
    Ao_d = nc.dram_tensor("Alo", [D, R], F32, kind="ExternalInput")
    gm_d = nc.dram_tensor("gm", [1, D], F32, kind="ExternalInput")
    bt_d = nc.dram_tensor("bt", [1, D], F32, kind="ExternalInput")
    fw_d = nc.dram_tensor("fw", [D, D], F32, kind="ExternalInput")
    fb_d = nc.dram_tensor("fb", [1, D], F32, kind="ExternalInput")
    al_d = nc.dram_tensor("al", [1, D], F32, kind="ExternalInput")
    out_d = nc.dram_tensor("out", [T, D], F32, kind="ExternalOutput")

    s_qT = nc.dram_tensor("s_qT", [D, T], BF16, kind="Internal")
    s_kT = nc.dram_tensor("s_kT", [D, T], BF16, kind="Internal")
    s_vT = nc.dram_tensor("s_vT", [D, T], BF16, kind="Internal")
    s_a = nc.dram_tensor("s_a", [D, T], BF16, kind="Internal")
    s_o = nc.dram_tensor("s_o", [D, T], BF16, kind="Internal")

    eye_d = nc.inline_tensor(np.eye(128, dtype=ml_dtypes.bfloat16), name="eye128")

    CONST_VALS = [EPS, -0.5, LORA_SCALING, INV_D, NEG_INV_D, SCALE, -1.0, 1.002]
    cst_tile = nc.alloc_sbuf_tensor("cstcols", [128, len(CONST_VALS)], F32)
    for i, v in enumerate(CONST_VALS):
        nc.const_aps.register(F32, float(v), cst_tile.ap()[:, i:i + 1])

    ident = nc.alloc_sbuf_tensor("ident", [128, 128], BF16)
    ones_f32r = nc.alloc_sbuf_tensor("ones_f32r", [1, 128], F32)
    ones_bfr = nc.alloc_sbuf_tensor("ones_bfr", [1, 128], BF16)
    ones_bfc = nc.alloc_sbuf_tensor("ones_bfc", [128, 1], BF16)
    ATqkv = nc.alloc_sbuf_tensor("ATqkv", [128, D], BF16)
    ATo = nc.alloc_sbuf_tensor("ATo", [16, D], BF16)
    BsT = nc.alloc_sbuf_tensor("BsT", [128, NCH * 80], BF16)
    BoT = nc.alloc_sbuf_tensor("BoT", [128, NCH * 16], BF16)
    bqs_c = nc.alloc_sbuf_tensor("bqs_c", [128, NCH], F32)
    bk_c = nc.alloc_sbuf_tensor("bk_c", [128, NCH], F32)
    bv_c = nc.alloc_sbuf_tensor("bv_c", [128, NCH], F32)

    sems = {}

    def S(name):
        if name not in sems:
            sems[name] = Sem(nc, name)
        return sems[name]

    band = {0: 0, 1: 32, 2: 64}
    PHASES = []

    es_xn = ExitStack()
    xn_fm = es_xn.enter_context(nc.sbuf_tensor("xn_fm", [128, NCH * T], BF16))

    es_p1 = ExitStack()
    gb_g = es_p1.enter_context(nc.sbuf_tensor("gb_g", [128, D], F32))
    gb_b = es_p1.enter_context(nc.sbuf_tensor("gb_b", [128, D], F32))
    xf = [es_p1.enter_context(nc.sbuf_tensor(f"xf{i}", [128, D], F32)) for i in range(2)]
    xn32 = es_p1.enter_context(nc.sbuf_tensor("xn32", [128, D], F32))
    xnbf = es_p1.enter_context(nc.sbuf_tensor("xnbf", [128, D], BF16))
    abf = es_p1.enter_context(nc.sbuf_tensor("abf", [128, D], BF16))
    a_acc = es_p1.enter_context(nc.sbuf_tensor("a_acc", [128, NCH * 256], BF16))
    stat = es_p1.enter_context(nc.sbuf_tensor("stat", [128, 8], F32))
    stat2 = es_p1.enter_context(nc.sbuf_tensor("stat2", [128, 8], F32))
    ptrA = es_p1.enter_context(nc.psum_tensor("ptrA", [128, 1024], BF16))
    ptrB = es_p1.enter_context(nc.psum_tensor("ptrB", [128, 1024], BF16))

    es_b0 = ExitStack()
    braw1 = es_b0.enter_context(nc.sbuf_tensor("braw1", [16, D // 2], F32))
    brawb1 = es_b0.enter_context(nc.sbuf_tensor("brawb1", [16, D // 2], BF16))
    araw1 = es_b0.enter_context(nc.sbuf_tensor("araw1", [128, (NCH // 2) * 16], F32))
    arawb1 = es_b0.enter_context(nc.sbuf_tensor("arawb1", [128, (NCH // 2) * 16], BF16))
    p_t16 = es_b0.enter_context(nc.psum_tensor("p_t16", [128, 1024], BF16))

    def _phase0(sync):
        ld = S("b0_ld")
        gp = S("b0_gp")
        pe = S("b0_pe")
        dv = S("b0_dv")
        ac = S("b0_ac")

        for i, v in enumerate(CONST_VALS):
            nc.gpsimd.memset(cst_tile.ap()[:, i:i + 1], float(v))
        nc.gpsimd.memset(ones_f32r.ap(), 1.0)
        nc.gpsimd.memset(ones_bfr.ap(), 1.0)
        nc.gpsimd.memset(ones_bfc.ap(), 1.0)
        nc.gpsimd.memset(BsT.ap(), 0.0)
        gp.inc(nc.gpsimd.memset(ATqkv.ap(), 0.0))

        ld.dinc(sync.dma_start(ident.ap(), eye_d.ap()))
        for c in range(NCH):
            ld.dinc(sync.dma_start(bqs_c.ap()[:, c:c + 1], bq_d.ap()[c * 128:(c + 1) * 128]))
            ld.dinc(sync.dma_start(bk_c.ap()[:, c:c + 1], bk_d.ap()[c * 128:(c + 1) * 128]))
            ld.dinc(sync.dma_start(bv_c.ap()[:, c:c + 1], bv_d.ap()[c * 128:(c + 1) * 128]))
        nc.scalar.wait_ge(ld.h, ld.v)
        nc.scalar.wait_ge(gp.h, gp.v)
        ac.inc(nc.scalar.mul(bqs_c.ap(), bqs_c.ap(), SCALE))

        ld2 = S("b0_ld2")
        nc.tensor.wait_ge(ld.h, ld.v)
        HCH = NCH // 2
        for i, bd in enumerate([Bq_d, Bk_d, Bv_d, Bo_d]):
            for hf in range(2):
                if dv.v:
                    sync.wait_ge(dv.h, dv.v)
                lv = ld2.dinc(sync.dma_start(
                    braw1.ap(), bd.ap()[:, hf * (D // 2):(hf + 1) * (D // 2)]))
                nc.vector.wait_ge(ld2.h, lv)
                if ld2.v > 16:
                    nc.vector.wait_ge(pe.h, pe.v)
                cv = dv.inc(nc.vector.tensor_copy(brawb1.ap(), braw1.ap()))
                nc.tensor.wait_ge(dv.h, cv)
                nc.tensor.wait_ge(gp.h, gp.v)
                for cc in range(HCH):
                    c = hf * HCH + cc
                    ins = nc.tensor.transpose(p_t16[:, cc * 16:(cc + 1) * 16],
                                              brawb1[:, cc * 128:(cc + 1) * 128],
                                              ident[0:16, 0:16])
                    if cc == HCH - 1:
                        pe.inc(ins)
                nc.vector.wait_ge(pe.h, pe.v)
                for cc in range(HCH):
                    c = hf * HCH + cc
                    if i < 3:
                        dst = BsT[:, c * 80 + band[i]: c * 80 + band[i] + 16]
                    else:
                        dst = BoT[:, c * 16:(c + 1) * 16]
                    ins = nc.vector.tensor_copy(dst, p_t16[:, cc * 16:(cc + 1) * 16])
                    if cc == HCH - 1:
                        dv.inc(ins)
                nc.tensor.wait_ge(dv.h, dv.v)

        for i, ad in enumerate([Aq_d, Ak_d, Av_d, Ao_d]):
            for hf in range(2):
                c0 = hf * HCH
                sync.wait_ge(dv.h, dv.v)
                lv = ld2.dinc(sync.dma_start(
                    araw1.ap().rearrange("p (c r) -> p c r", r=R),
                    ad.ap().rearrange("(c p) r -> p c r", p=128)[:, c0:c0 + HCH, :]))
                nc.vector.wait_ge(ld2.h, lv)
                nc.vector.wait_ge(pe.h, pe.v)
                cv = dv.inc(nc.vector.tensor_copy(arawb1.ap(), araw1.ap()))
                nc.tensor.wait_ge(dv.h, cv)
                for cc in range(HCH):
                    ins = nc.tensor.transpose(p_t16[0:16, cc * 128:(cc + 1) * 128],
                                              arawb1[:, cc * 16:(cc + 1) * 16],
                                              ident[:])
                    if cc == HCH - 1:
                        pe.inc(ins)
                nc.vector.wait_ge(pe.h, pe.v)
                for cc in range(HCH):
                    c = c0 + cc
                    if i < 3:
                        dst = ATqkv[band[i]:band[i] + 16, c * 128:(c + 1) * 128]
                    else:
                        dst = ATo[0:16, c * 128:(c + 1) * 128]
                    ins = nc.vector.tensor_copy(dst, p_t16[0:16, cc * 128:(cc + 1) * 128])
                    if cc == HCH - 1:
                        dv.inc(ins)
                nc.tensor.wait_ge(dv.h, dv.v)

    PHASES.append(_phase0)
    es_b0.close()


    xn_v = xn_fm.ap().rearrange("p (c n) -> p c n", n=T)
    aac_v = a_acc.ap().rearrange("p (c n) -> p c n", n=256)

    def _phase1(sync):
        ldx = [S("p1_ldx0"), S("p1_ldx1")]
        dv = S("p1_dv")
        ac = S("p1_ac")
        pe = S("p1_pe")
        sta = S("p1_sta")
        evA = S("p1_evA")
        evB = S("p1_evB")
        abf_free = [0]
        xnbf_free = [0]
        vxn = {}
        sq1 = {}
        nmr1 = {}
        sq2_prev = [0]

        def stats_head(src_ap, scratch, sb, act_dv_wait=None, act_pe_wait=None,
                       dv_wait=None, mid=None, nm_ac_wait=None):
            if dv_wait is not None:
                nc.vector.wait_ge(dv.h, dv_wait)
            v_sum = dv.inc(nc.vector.reduce_sum(sb[:, 0:1], src_ap, axis=AX.X))
            nc.vector.wait_ge(dv.h, v_sum)
            v_mu = dv.inc(nc.vector.tensor_scalar_mul(sb[:, 2:3], sb[:, 0:1], INV_D))
            nc.vector.wait_ge(dv.h, v_mu)
            if nm_ac_wait:
                nc.vector.wait_ge(ac.h, nm_ac_wait)
            v_nm = dv.inc(nc.vector.tensor_scalar_mul(sb[:, 3:4], sb[:, 2:3], -1.0))
            if act_pe_wait is not None:
                nc.scalar.wait_ge(pe.h, act_pe_wait)
            if act_dv_wait is not None:
                nc.scalar.wait_ge(dv.h, act_dv_wait)
            nc.scalar.wait_ge(dv.h, v_nm)
            a_sq = ac.inc(nc.scalar.activation(scratch, src_ap, AF.Square,
                                               bias=sb[:, 3:4],
                                               accum_out=sb[:, 1:2]))
            if mid is not None:
                mid()
            return a_sq

        def stats_tail(sb, a_sq):
            nc.vector.wait_ge(ac.h, a_sq)
            v_var = dv.inc(nc.vector.tensor_scalar_mul(sb[:, 4:5], sb[:, 1:2], INV_D))
            nc.scalar.wait_ge(dv.h, v_var)
            a_ln = ac.inc(nc.scalar.activation(sb[:, 7:8], sb[:, 4:5], AF.Ln,
                                               bias=EPS))
            nc.scalar.wait_ge(ac.h, a_ln)
            a_r = ac.inc(nc.scalar.activation(sb[:, 5:6], sb[:, 7:8], AF.Exp,
                                              scale=-0.5))
            nc.vector.wait_ge(ac.h, a_r)
            v_nmr = dv.inc(nc.vector.tensor_scalar(
                sb[:, 6:7], sb[:, 2:3], sb[:, 5:6], -1.0, ALU.mult, ALU.mult))
            return v_nmr

        def ln1_head(t, pe_wait=None):
            s = t % 2
            nc.vector.wait_ge(ldx[s].h, ldx[s].v)
            nc.scalar.wait_ge(ldx[s].h, ldx[s].v)
            return stats_head(
                xf[s].ap(), xf[1 - s].ap(), stat,
                act_dv_wait=(vxn[t - 1] if t >= 1 else None),
                act_pe_wait=pe_wait,
                nm_ac_wait=(sq1[t - 1] if t >= 1 else 0))

        gld = S("p1_gld")
        gld.dinc(sync.dma_start(xf[0].ap()[0:1, :], gm_d.ap()))
        gld.dinc(sync.dma_start(xf[1].ap()[0:1, :], bt_d.ap()))
        nc.tensor.wait_ge(gld.h, gld.v)
        for src_row, dst in ((xf[0], gb_g), (xf[1], gb_b)):
            for j in range(D // 512):
                if dv.v:
                    nc.tensor.wait_ge(dv.h, dv.v)
                pe.inc(nc.tensor.matmul(
                    ptrA[:, 0:1024].bitcast(F32), ones_f32r.ap(),
                    src_row.ap()[0:1, j * 512:(j + 1) * 512], start=True, stop=True))
                nc.vector.wait_ge(pe.h, pe.v)
                dv.inc(nc.vector.tensor_copy(dst[:, j * 512:(j + 1) * 512],
                                             ptrA[:, 0:1024].bitcast(F32)))
        nc.vector.wait_ge(dv.h, dv.v)

        sync.wait_ge(pe.h, pe.v)
        ldx[0].dinc(sync.dma_start(xf[0].ap(), x_d.ap()[0:128, :]))

        sq1[0] = ln1_head(0, pe_wait=pe.v)
        nmr1[0] = stats_tail(stat, sq1[0])
        nc.vector.wait_ge(dv.h, nmr1[0])
        vxn[0] = dv.inc(nc.vector.tensor_scalar(
            xn32.ap(), xf[0].ap(), stat[:, 5:6], stat[:, 6:7],
            ALU.mult, ALU.add))

        for t in range(NT):
            sl = t % 2
            if t + 1 < NT:
                if t >= 1:
                    sync.wait_ge(dv.h, vxn[t - 1])
                sync.wait_ge(ac.h, sq1[t])
                ldx[(t + 1) % 2].dinc(sync.dma_start(
                    xf[(t + 1) % 2].ap(), x_d.ap()[(t + 1) * 128:(t + 2) * 128, :]))

            xnbf_cp = [0]

            def _cast():
                nc.scalar.wait_ge(ac.h, ac.v)
                xnbf_cp[0] = ac.inc(nc.scalar.activation(xnbf.ap(), xn32.ap(),
                                                         AF.Copy))
            a_sq2 = stats_head(xn32.ap(), xnbf.ap(), stat2,
                               act_dv_wait=vxn[t], act_pe_wait=xnbf_free[0],
                               dv_wait=vxn[t], mid=_cast,
                               nm_ac_wait=sq2_prev[0])
            sq2_prev[0] = a_sq2
            if t + 1 < NT:
                sq1[t + 1] = ln1_head(t + 1)
            v_nmr2 = stats_tail(stat2, a_sq2)

            nc.vector.wait_ge(dv.h, vxn[t])
            nc.vector.wait_ge(dv.h, v_nmr2)
            v_u = dv.inc(nc.vector.tensor_scalar(
                xn32.ap(), xn32.ap(), stat2[:, 5:6], stat2[:, 6:7],
                ALU.mult, ALU.add))
            nc.vector.wait_ge(dv.h, v_u)
            v_g = dv.inc(nc.vector.tensor_tensor(xn32.ap(), xn32.ap(), gb_g.ap(), ALU.mult))
            nc.vector.wait_ge(dv.h, v_g)
            v_b = dv.inc(nc.vector.tensor_tensor(xn32.ap(), xn32.ap(), gb_b.ap(), ALU.add))
            nc.scalar.wait_ge(dv.h, v_b)
            nc.scalar.wait_ge(pe.h, abf_free[0])
            a_s = ac.inc(nc.scalar.activation(abf.ap(), xn32.ap(), AF.Sign))
            if t + 1 < NT:
                nmr1[t + 1] = stats_tail(stat, sq1[t + 1])

            nhalf = min(8, NCH)
            nc.tensor.wait_ge(ac.h, xnbf_cp[0])
            nc.tensor.wait_ge(evA.h, evA.v)
            for j in range(nhalf):
                ins = nc.tensor.transpose(ptrA[:, j * 128:(j + 1) * 128],
                                          xnbf[:, j * 128:(j + 1) * 128], ident[:])
                if j == nhalf - 1:
                    pe.inc(ins)
            nc.scalar.wait_ge(pe.h, pe.v)
            evA.inc(nc.scalar.activation(
                xn_v[:, 0:nhalf, t * 128:(t + 1) * 128],
                ptrA[:].rearrange("p (c n) -> p c n", n=128)[:, 0:nhalf, :],
                AF.Copy))
            if NCH > 8:
                nc.tensor.wait_ge(evB.h, evB.v)
                for j in range(8, NCH):
                    ins = nc.tensor.transpose(ptrB[:, (j - 8) * 128:(j - 7) * 128],
                                              xnbf[:, j * 128:(j + 1) * 128], ident[:])
                    if j == NCH - 1:
                        pe.inc(ins)
                nc.scalar.wait_ge(pe.h, pe.v)
                evB.inc(nc.scalar.activation(
                    xn_v[:, 8:NCH, t * 128:(t + 1) * 128],
                    ptrB[:].rearrange("p (c n) -> p c n", n=128)[:, 0:NCH - 8, :],
                    AF.Copy))
            xnbf_free[0] = pe.v

            nc.tensor.wait_ge(ac.h, a_s)
            nc.tensor.wait_ge(evA.h, evA.v)
            for j in range(nhalf):
                ins = nc.tensor.transpose(ptrA[:, j * 128:(j + 1) * 128],
                                          abf[:, j * 128:(j + 1) * 128], ident[:])
                if j == nhalf - 1:
                    pe.inc(ins)
            nc.scalar.wait_ge(pe.h, pe.v)
            if t % 2 == 0 and t >= 2:
                nc.scalar.wait_ge(sta.h, sta.v)
            evA.inc(nc.scalar.activation(
                aac_v[:, 0:nhalf, (t % 2) * 128:(t % 2) * 128 + 128],
                ptrA[:].rearrange("p (c n) -> p c n", n=128)[:, 0:nhalf, :],
                AF.Copy))
            if NCH > 8:
                nc.tensor.wait_ge(evB.h, evB.v)
                for j in range(8, NCH):
                    ins = nc.tensor.transpose(ptrB[:, (j - 8) * 128:(j - 7) * 128],
                                              abf[:, j * 128:(j + 1) * 128], ident[:])
                    if j == NCH - 1:
                        pe.inc(ins)
                nc.scalar.wait_ge(pe.h, pe.v)
                evB.inc(nc.scalar.activation(
                    aac_v[:, 8:NCH, (t % 2) * 128:(t % 2) * 128 + 128],
                    ptrB[:].rearrange("p (c n) -> p c n", n=128)[:, 0:NCH - 8, :],
                    AF.Copy))

            abf_free[0] = pe.v

            if t % 2 == 1:
                nc.scalar.wait_ge(evA.h, evA.v)
                if NCH > 8:
                    nc.scalar.wait_ge(evB.h, evB.v)
                sta.dinc(nc.scalar.dma_start(
                    s_a.ap().rearrange("(c p) n -> p c n", p=128)[:, :,
                                                                 (t - 1) * 128:(t + 1) * 128],
                    aac_v))

            if t + 1 < NT:
                nc.vector.wait_ge(dv.h, nmr1[t + 1])
                nc.vector.wait_ge(ac.h, a_s)
                vxn[t + 1] = dv.inc(nc.vector.tensor_scalar(
                    xn32.ap(), xf[(t + 1) % 2].ap(), stat[:, 5:6], stat[:, 6:7],
                    ALU.mult, ALU.add))

        sync.wait_ge(ldx[0].h, ldx[0].v)
        sync.wait_ge(ldx[1].h, ldx[1].v)
        nc.scalar.wait_ge(sta.h, sta.v)

    PHASES.append(_phase1)
    es_p1.close()

    es_p2 = ExitStack()
    wbuf = [es_p2.enter_context(nc.sbuf_tensor(f"wbuf{i}", [128, D], F32)) for i in range(2)]
    wbf = [es_p2.enter_context(nc.sbuf_tensor(f"wbf{i}", [128, D], BF16)) for i in range(2)]
    wT = [es_p2.enter_context(nc.sbuf_tensor(f"wT{i}", [128, D], BF16)) for i in range(2)]
    stg = [es_p2.enter_context(nc.sbuf_tensor(f"stg{i}", [128, 1536], BF16)) for i in range(2)]
    xBs = es_p2.enter_context(nc.sbuf_tensor("xBs", [128, T], BF16))
    pgA = es_p2.enter_context(nc.psum_tensor("pgA", [128, 1536], F32))
    pgB = es_p2.enter_context(nc.psum_tensor("pgB", [128, 1536], F32))
    ptrW = es_p2.enter_context(nc.psum_tensor("ptrW", [128, 1024], BF16))

    chunk_groups = []
    s0 = 0
    while s0 < NQC:
        n = min(3, NQC - s0)
        chunk_groups.append((s0, n))
        s0 += n
    NG = len(chunk_groups)

    def _phase2(sync):
        ldw = [S("p2_ldw0"), S("p2_ldw1")]
        pe = S("p2_pe")
        dv = S("p2_dv")
        ac = S("p2_ac")
        stq2 = [S("p2_stq0"), S("p2_stq1")]
        cast_v = {0: 0, 1: 0}
        wfree_v = {0: 0, 1: 0}
        pgev_v = {0: None, 1: None}
        stgf_v = {0: 0, 1: 0}

        for tc in range(NQC):
            if tc >= 1:
                nc.tensor.wait_ge(ac.h, ac.v)
            for ic in range(NCH):
                pe.inc(nc.tensor.matmul(
                    pgA[0:80, 0:512],
                    BsT[:, ic * 80:(ic + 1) * 80],
                    xn_fm[:, ic * T + tc * 512: ic * T + (tc + 1) * 512],
                    start=(ic == 0), stop=(ic == NCH - 1)))
            nc.scalar.wait_ge(pe.h, pe.v)
            ac.inc(nc.scalar.activation(xBs[0:80, tc * 512:(tc + 1) * 512],
                                        pgA[0:80, 0:512], AF.Copy,
                                        scale=LORA_SCALING))
        pgev_v[0] = (ac, ac.v)

        projs = [
            ("q", wq_d, s_qT, True, bqs_c, 0),
            ("k", wk_d, s_kT, False, bk_c, 32),
            ("v", wv_d, s_vT, False, bv_c, 64),
        ]
        ldw[0].dinc(sync.dma_start(wbuf[0].ap(), wq_d.ap()[0:128, :]))
        it = 0
        gidx = 0
        for pi, (pname, w_d, dst_d, scaled, bias_c, bandofs) in enumerate(projs):
            for och in range(NCH):
                sl = it % 2
                nxt = None
                if och + 1 < NCH:
                    nxt = (w_d, och + 1)
                elif pi + 1 < len(projs):
                    nxt = (projs[pi + 1][1], 0)
                if nxt is not None:
                    if it >= 1:
                        sync.wait_ge(dv.h, cast_v[(it + 1) % 2])
                    ldw[(it + 1) % 2].dinc(sync.dma_start(
                        wbuf[(it + 1) % 2].ap(),
                        nxt[0].ap()[nxt[1] * 128:(nxt[1] + 1) * 128, :]))

                nc.vector.wait_ge(ldw[sl].h, ldw[sl].v)
                if it >= 2:
                    nc.vector.wait_ge(pe.h, wfree_v[sl])
                cv = dv.inc(nc.vector.tensor_copy(wbf[sl].ap(), wbuf[sl].ap()))
                cast_v[sl] = cv

                nc.tensor.wait_ge(dv.h, cv)
                for half in range(0, NCH, 8):
                    nh = min(8, NCH - half)
                    if half > 0 or it > 0:
                        nc.tensor.wait_ge(dv.h, dv.v)
                    for j in range(nh):
                        ins = nc.tensor.transpose(
                            ptrW[:, j * 128:(j + 1) * 128],
                            wbf[sl][:, (half + j) * 128:(half + j + 1) * 128],
                            ident[:])
                        if j == nh - 1:
                            pe.inc(ins)
                    nc.vector.wait_ge(pe.h, pe.v)
                    dv.inc(nc.vector.tensor_copy(
                        wT[sl][:, half * 128:(half + nh) * 128],
                        ptrW[:, 0:nh * 128]))
                wT_ready = dv.v

                nc.tensor.wait_ge(dv.h, wT_ready)
                for (c0, ncks) in chunk_groups:
                    psl = gidx % 2
                    pg = pgA if psl == 0 else pgB
                    if pgev_v[psl] is not None:
                        psem, pval = pgev_v[psl]
                        nc.tensor.wait_ge(psem.h, pval)
                    for ic in range(NCH):
                        for k in range(ncks):
                            tc = c0 + k
                            pe.inc(nc.tensor.matmul(
                                pg[:, k * 512:(k + 1) * 512],
                                wT[sl][:, ic * 128:(ic + 1) * 128],
                                xn_fm[:, ic * T + tc * 512: ic * T + (tc + 1) * 512],
                                start=(ic == 0), stop=False,
                                skip_group_check=True))
                    for k in range(ncks):
                        tc = c0 + k
                        pe.inc(nc.tensor.matmul(
                            pg[:, k * 512:(k + 1) * 512],
                            ATqkv[bandofs:bandofs + 16, och * 128:(och + 1) * 128],
                            xBs[bandofs:bandofs + 16, tc * 512:(tc + 1) * 512],
                            start=False, stop=True, skip_group_check=True))
                    wfree_v[sl] = pe.v
                    nc.vector.wait_ge(pe.h, pe.v)
                    if stgf_v[psl]:
                        nc.vector.wait_ge(stq2[psl].h, stgf_v[psl])
                    v_ev = dv.inc(nc.vector.tensor_scalar(
                        stg[psl][:, 0:ncks * 512], pg[:, 0:ncks * 512],
                        (SCALE if scaled else 1.0),
                        bias_c.ap()[:, och:och + 1], ALU.mult, ALU.add))
                    pgev_v[psl] = (dv, v_ev)
                    nc.scalar.wait_ge(dv.h, v_ev)
                    stq2[psl].dinc(nc.scalar.dma_start(
                        dst_d.ap()[och * 128:(och + 1) * 128,
                                   c0 * 512:(c0 + ncks) * 512],
                        stg[psl][:, 0:ncks * 512]))
                    stgf_v[psl] = stq2[psl].v
                    gidx += 1
                it += 1

        sync.wait_ge(ldw[0].h, ldw[0].v)
        sync.wait_ge(ldw[1].h, ldw[1].v)
        nc.scalar.wait_ge(stq2[0].h, stq2[0].v)
        nc.scalar.wait_ge(stq2[1].h, stq2[1].v)

    PHASES.append(_phase2)
    es_p2.close()
    es_xn.close()

    es_p3 = ExitStack()
    kT_b = [es_p3.enter_context(nc.sbuf_tensor(f"kT{i}", [128, T], BF16)) for i in range(2)]
    qT_b = [es_p3.enter_context(nc.sbuf_tensor(f"qT{i}", [128, T], BF16)) for i in range(2)]
    vTf_b = [es_p3.enter_context(nc.sbuf_tensor(f"vTf{i}", [128, T], BF16)) for i in range(2)]
    vh2 = [es_p3.enter_context(nc.sbuf_tensor(f"vh{i}", [128, T], BF16)) for i in range(2)]
    pT2 = [es_p3.enter_context(nc.sbuf_tensor(f"pT{i}", [128, NT * 512], BF16)) for i in range(2)]
    sq3b = [es_p3.enter_context(nc.sbuf_tensor(f"sq3{i}", [128, T], BF16)) for i in range(2)]
    mxp = es_p3.enter_context(nc.sbuf_tensor("mxp", [1, NQC], F32))
    mxs = es_p3.enter_context(nc.sbuf_tensor("mxs", [1, 4], F32))
    negc2 = [es_p3.enter_context(nc.sbuf_tensor(f"negc{i}", [128, 1], F32)) for i in range(2)]
    rec2 = [es_p3.enter_context(nc.sbuf_tensor(f"rec2{i}", [1, 512], F32)) for i in range(2)]
    obuf = [es_p3.enter_context(nc.sbuf_tensor(f"obuf{i}", [128, 512], F32)) for i in range(2)]
    ptree2 = [es_p3.enter_context(nc.sbuf_tensor(f"ptree{i}", [128, NT * 256], BF16)) for i in range(2)]
    ostg = [es_p3.enter_context(nc.sbuf_tensor(f"ostg{i}", [128, 512], BF16)) for i in range(2)]
    sg3 = [es_p3.enter_context(nc.psum_tensor(f"sg3{i}", [128, 1024], F32)) for i in range(3)]
    avp = [es_p3.enter_context(nc.psum_tensor(f"avp{i}", [128, 512], F32)) for i in range(2)]

    def _phase3(sync):
        ldk = [S("p3_ldk0"), S("p3_ldk1")]
        ldq = [S("p3_ldq0"), S("p3_ldq1")]
        ldv = [S("p3_ldv0"), S("p3_ldv1")]
        pe = S("p3_pe")
        dv = S("p3_dv")
        ac = S("p3_ac")
        sto3 = [S("p3_sto0"), S("p3_sto1")]
        av_done = 0
        av_done_slot = {0: 0, 1: 0}
        ost_done = {0: 0, 1: 0}
        avfree = {0: 0, 1: 0}
        rot = {"i": 0}
        bank_waits = {0: [], 1: [], 2: []}
        sq3_free = {0: 0, 1: 0}
        vh_slot_free = {0: 0, 1: 0}
        tree_a = [0]
        negc_ready = {0: 0, 1: 0}


        def take_bank():
            b = rot["i"] % 3
            rot["i"] += 1
            for (sem, val) in bank_waits[b]:
                nc.tensor.wait_ge(sem.h, val)
            bank_waits[b] = []
            return b

        pending_sums = None
        pending_fin = None

        def emit_fin_stage(pend, cur_qsl):
            ph, pq0, pqsl, pv_ob = pend
            nc.tensor.wait_ge(dv.h, pv_ob)
            nc.tensor.wait_ge(dv.h, rec_ready[pqsl])
            pv = pe.inc(nc.tensor.matmul(avp[cur_qsl][:], ones_f32r.ap(),
                                         rec2[pqsl][0:1, :], start=True, stop=True))
            rec_read[pqsl] = pv
            nc.vector.wait_ge(pe.h, pv)
            nc.vector.wait_ge(dv.h, pv_ob)
            if ost_done[pqsl]:
                nc.vector.wait_ge(sto3[pqsl].h, ost_done[pqsl])
            v_os = dv.inc(nc.vector.tensor_tensor(ostg[pqsl].ap(), obuf[pqsl].ap(),
                                                  avp[cur_qsl][:], ALU.mult))
            avfree[cur_qsl] = v_os
            nc.scalar.wait_ge(dv.h, v_os)
            sto3[pqsl].dinc(nc.scalar.dma_start(
                s_o.ap()[ph * 128:(ph + 1) * 128, pq0:pq0 + 512],
                ostg[pqsl].ap()))
            ost_done[pqsl] = sto3[pqsl].v

        def emit_sums_stage(pend, cur_qsl):
            pqsl, pv_part, ph_s, pq0_s = pend
            nc.vector.wait_ge(pe.h, av_done)
            v_ob = dv.inc(nc.vector.tensor_copy(obuf[pqsl].ap(), avp[pqsl][:]))
            nc.tensor.wait_ge(dv.h, pv_part)
            nc.tensor.wait_ge(dv.h, v_ob)
            ptree_mm[pqsl] = pe.inc(nc.tensor.matmul(
                avp[pqsl][0:1, 0:512], ones_bfc.ap(), ptree2[pqsl][:, 0:512],
                start=True, stop=True, skip_group_check=True))
            if rec_read[pqsl]:
                nc.vector.wait_ge(pe.h, rec_read[pqsl])
            nc.vector.wait_ge(pe.h, ptree_mm[pqsl])
            v_rec = dv.inc(nc.vector.reciprocal(rec2[pqsl][0:1, :],
                                                avp[pqsl][0:1, 0:512]))
            rec_ready[pqsl] = v_rec
            avfree[pqsl] = v_rec
            return (ph_s, pq0_s, pqsl, v_ob)

        ptree_mm = {0: 0, 1: 0}
        rec_ready = {0: 0, 1: 0}
        rec_read = {0: 0, 1: 0}
        vpart_slot = {0: 0, 1: 0}
        sumbank = {}

        def emit_vtrans(hh):
            psl = hh % 2
            nc.tensor.wait_ge(ldv[psl].h, ldv[psl].v)
            if vh_slot_free[psl]:
                nc.tensor.wait_ge(pe.h, vh_slot_free[psl])
            for kt0 in range(0, NT, 16):
                nb = min(16, NT - kt0)
                b = take_bank()
                bview = sg3[b][:].bitcast(BF16)
                for j in range(nb):
                    kt = kt0 + j
                    ins = nc.tensor.transpose(bview[:, j * 128:(j + 1) * 128],
                                              vTf_b[psl][:, kt * 128:(kt + 1) * 128],
                                              ident[:])
                    if j == nb - 1:
                        pe.inc(ins)
                nc.vector.wait_ge(pe.h, pe.v)
                ev = dv.inc(nc.vector.tensor_copy(vh2[psl][:, kt0 * 128:(kt0 + nb) * 128],
                                                  bview[:, 0:nb * 128]))
                bank_waits[b].append((dv, ev))

        def emit_norm(hh):
            psl = hh % 2
            nc.scalar.wait_ge(ldk[psl].h, ldk[psl].v)
            nc.scalar.wait_ge(ldq[psl].h, ldq[psl].v)
            if sq3_free[psl]:
                nc.scalar.wait_ge(pe.h, sq3_free[psl])
            for (srcb, col) in ((kT_b[psl], 1), (qT_b[psl], 0)):
                if col == 0:
                    nc.scalar.wait_ge(pe.h, sq3_free[psl])
                aq = ac.inc(nc.scalar.activation(sq3b[psl].ap(), srcb.ap(), AF.Square))
                nc.tensor.wait_ge(ac.h, aq)
                nc.vector.wait_ge(dv.h, dv.v)
                for tc in range(NQC):
                    b = take_bank()
                    pe.inc(nc.tensor.matmul(
                        sg3[b][0:1, 0:512], ones_bfc.ap(),
                        sq3b[psl][:, tc * 512:(tc + 1) * 512], start=True, stop=True))
                    nc.vector.wait_ge(pe.h, pe.v)
                    rv = dv.inc(nc.vector.reduce_max(mxp[0:1, tc:tc + 1],
                                                     sg3[b][0:1, 0:512], axis=AX.X))
                    bank_waits[b].append((dv, rv))
                sq3_free[psl] = pe.v
                nc.vector.wait_ge(dv.h, dv.v)
                dv.inc(nc.vector.reduce_max(mxs[0:1, col:col + 1], mxp[0:1, 0:NQC],
                                            axis=AX.X))
            nc.vector.wait_ge(dv.h, dv.v)
            v_m2 = dv.inc(nc.vector.tensor_tensor(mxs[0:1, 2:3], mxs[0:1, 0:1],
                                                  mxs[0:1, 1:2], ALU.mult))
            nc.scalar.wait_ge(dv.h, v_m2)
            a_l = ac.inc(nc.scalar.activation(mxs[0:1, 3:4], mxs[0:1, 2:3], AF.Ln,
                                              scale=1.002))
            nc.scalar.wait_ge(ac.h, a_l)
            a_c = ac.inc(nc.scalar.activation(mxs[0:1, 3:4], mxs[0:1, 3:4], AF.Exp,
                                              scale=LORA_SCALING))
            nc.scalar.wait_ge(ac.h, a_c)
            a_cn = ac.inc(nc.scalar.mul(mxs[0:1, 3:4], mxs[0:1, 3:4], -1.0))
            b = take_bank()
            nc.tensor.wait_ge(ac.h, a_cn)
            pv = pe.inc(nc.tensor.matmul(sg3[b][:, 0:1], ones_f32r.ap(),
                                         mxs[0:1, 3:4], start=True, stop=True))
            nc.vector.wait_ge(pe.h, pv)
            v_nc = dv.inc(nc.vector.tensor_copy(negc2[psl].ap(), sg3[b][:, 0:1]))
            bank_waits[b].append((dv, v_nc))
            negc_ready[psl] = v_nc

        ldk[0].dinc(sync.dma_start(kT_b[0].ap(), s_kT.ap()[0:128, :]))
        ldq[0].dinc(sync.dma_start(qT_b[0].ap(), s_qT.ap()[0:128, :]))
        ldv[0].dinc(sync.dma_start(vTf_b[0].ap(), s_vT.ap()[0:128, :]))
        emit_vtrans(0)
        emit_norm(0)

        for h in range(H):
            sl = h % 2
            if h + 1 < H:
                if h >= 1:
                    sync.wait_ge(pe.h, av_done)
                r0 = (h + 1) * 128
                ldk[(h + 1) % 2].dinc(sync.dma_start(kT_b[(h + 1) % 2].ap(),
                                                     s_kT.ap()[r0:r0 + 128, :]))
                ldq[(h + 1) % 2].dinc(sync.dma_start(qT_b[(h + 1) % 2].ap(),
                                                     s_qT.ap()[r0:r0 + 128, :]))
                ldv[(h + 1) % 2].dinc(sync.dma_start(vTf_b[(h + 1) % 2].ap(),
                                                     s_vT.ap()[r0:r0 + 128, :]))

            if h >= 1:
                emit_vtrans(h)
                emit_norm(h)

            s_done_cur = []
            s_bank_cur = []
            for qc in range(NQC):
                qsl = qc % 2
                q0 = qc * 512

                if pending_sums is not None:
                    fin2 = emit_sums_stage(pending_sums, qsl)
                    pending_sums = None
                else:
                    fin2 = None
                if pending_fin is not None:
                    assert pending_fin[2] == qsl, (h, qc, pending_fin)
                    emit_fin_stage(pending_fin, qsl)
                pending_fin = fin2

                def emit_S(g, dstv, dstb, qq0):
                    b = take_bank()
                    for j in range(2):
                        kt = 2 * g + j
                        ins = nc.tensor.matmul(
                            sg3[b][:, j * 512:(j + 1) * 512],
                            kT_b[sl][:, kt * 128:(kt + 1) * 128],
                            qT_b[sl][:, qq0:qq0 + 512],
                            start=True, stop=True, skip_group_check=True)
                        if j == 1:
                            pe.inc(ins)
                    dstv.append(pe.v)
                    dstb.append(b)

                if qc == 0:
                    s_done_cur, s_bank_cur = [], []
                    for g0 in range(min(3, NKG)):
                        emit_S(g0, s_done_cur, s_bank_cur, q0)
                s_done, s_bank = s_done_cur, s_bank_cur
                for g in range(NKG):
                    nc.scalar.wait_ge(pe.h, s_done[g])
                    if g == 0:
                        nc.scalar.wait_ge(pe.h, av_done_slot[qsl])
                        if vpart_slot[qsl]:
                            nc.scalar.wait_ge(dv.h, vpart_slot[qsl])
                        if qc == 0:
                            nc.scalar.wait_ge(dv.h, negc_ready[sl])
                    e = ac.inc(nc.scalar.activation(
                        pT2[qsl][:, g * 1024:(g + 1) * 1024], sg3[s_bank[g]][:],
                        AF.Exp, bias=negc2[sl].ap()))
                    bank_waits[s_bank[g]].append((ac, e))
                    nc.tensor.wait_ge(ac.h, e)
                    if g == 0:
                        if avfree[qsl]:
                            nc.tensor.wait_ge(dv.h, avfree[qsl])
                    if g + 3 < NKG:
                        emit_S(g + 3, s_done, s_bank, q0)
                    for j in range(2):
                        kt = 2 * g + j
                        pe.inc(nc.tensor.matmul(
                            avp[qsl][:], vh2[sl][:, kt * 128:(kt + 1) * 128],
                            pT2[qsl][:, kt * 512:(kt + 1) * 512],
                            start=(kt == 0), stop=(kt == NT - 1),
                            skip_group_check=True))
                    if g == (3 * NT // 4 - 1) // 2:
                        wq = NT * 256
                        if ptree_mm[qsl]:
                            nc.vector.wait_ge(pe.h, ptree_mm[qsl])
                        nc.vector.wait_ge(ac.h, e)
                        with nc.allow_low_precision(reason="bf16 tree partials (softmax sums; final 128-way sum in fp32 PSUM)"):
                            tree_a[0] = dv.inc(nc.vector.tensor_tensor(
                                ptree2[qsl][:, 0:wq // 2], pT2[qsl][:, 0:wq // 2],
                                pT2[qsl][:, wq:wq + wq // 2], ALU.add))
                if qc + 1 < NQC:
                    s_done_cur, s_bank_cur = [], []
                    for g0 in range(min(3, NKG)):
                        emit_S(g0, s_done_cur, s_bank_cur, (qc + 1) * 512)
                av_done = pe.v
                av_done_slot[qsl] = pe.v

                nc.vector.wait_ge(ac.h, ac.v)
                w = NT * 256
                with nc.allow_low_precision(reason="bf16 tree partials (softmax sums; final 128-way sum in fp32 PSUM)"):
                    v_part = dv.inc(nc.vector.tensor_tensor(
                        ptree2[qsl][:, w // 2:w], pT2[qsl][:, w // 2:w],
                        pT2[qsl][:, w + w // 2:2 * w], ALU.add))
                    while w > 512:
                        nc.vector.wait_ge(dv.h, v_part)
                        if tree_a[0]:
                            nc.vector.wait_ge(dv.h, tree_a[0])
                            tree_a[0] = 0
                        v_part = dv.inc(nc.vector.tensor_tensor(
                            ptree2[qsl][:, 0:w // 2], ptree2[qsl][:, 0:w // 2],
                            ptree2[qsl][:, w // 2:w], ALU.add))
                        w //= 2
                vpart_slot[qsl] = v_part
                pending_sums = (qsl, v_part, h, q0)
            vh_slot_free[sl] = av_done

        if pending_sums is not None:
            fin2 = emit_sums_stage(pending_sums, 1 - pending_sums[0])
        else:
            fin2 = None
        if pending_fin is not None:
            emit_fin_stage(pending_fin, pending_fin[2])
        if fin2 is not None:
            emit_fin_stage(fin2, fin2[2])

        sync.wait_ge(ldk[0].h, ldk[0].v)
        sync.wait_ge(ldk[1].h, ldk[1].v)
        sync.wait_ge(ldq[0].h, ldq[0].v)
        sync.wait_ge(ldq[1].h, ldq[1].v)
        sync.wait_ge(ldv[0].h, ldv[0].v)
        sync.wait_ge(ldv[1].h, ldv[1].v)
        nc.scalar.wait_ge(sto3[0].h, sto3[0].v)
        nc.scalar.wait_ge(sto3[1].h, sto3[1].v)

    PHASES.append(_phase3)
    es_p3.close()

    es_p4a = ExitStack()
    woT = es_p4a.enter_context(nc.sbuf_tensor("woT", [128, NCH * D], BF16))
    wbuf4 = [es_p4a.enter_context(nc.sbuf_tensor(f"wbuf4{i}", [128, D], F32)) for i in range(2)]
    wbf4 = es_p4a.enter_context(nc.sbuf_tensor("wbf4", [128, D], BF16))
    o_sp = [es_p4a.enter_context(nc.sbuf_tensor(f"o_sp{i}", [128, NCH * 512], BF16)) for i in range(2)]
    x_sp = [es_p4a.enter_context(nc.sbuf_tensor(f"x_sp{i}", [128, D], F32)) for i in range(2)]
    res4 = [es_p4a.enter_context(nc.sbuf_tensor(f"res4{i}", [128, 1024], F32)) for i in range(2)]
    oBs4 = es_p4a.enter_context(nc.sbuf_tensor("oBs4", [16, 512], BF16))
    bo_row = es_p4a.enter_context(nc.sbuf_tensor("bo_row", [1, D], BF16))
    bo_row32 = es_p4a.enter_context(nc.sbuf_tensor("bo_row32", [1, D], F32))
    pa = [es_p4a.enter_context(nc.psum_tensor(f"pa{i}", [128, 1024], F32)) for i in range(2)]
    pob = es_p4a.enter_context(nc.psum_tensor("pob", [128, 512], F32))
    ptr4 = es_p4a.enter_context(nc.psum_tensor("ptr4", [128, 1024], BF16))

    NSP = T // 512
    woT_v = woT.ap().rearrange("p (c n) -> p c n", n=D)
    so_v = s_o.ap().rearrange("(c p) t -> p c t", p=128)

    def _phase4(sync):
        ldw = [S("p4a_ldw0"), S("p4a_ldw1")]
        ldo = [S("p4a_ldo0"), S("p4a_ldo1")]
        ldx4 = [S("p4a_ldx0"), S("p4a_ldx1")]
        pe = S("p4a_pe")
        dv = S("p4a_dv")
        ac = S("p4a_ac")
        sto4 = [S("p4a_sto0"), S("p4a_sto1")]
        resfree = {0: 0, 1: 0}
        res_spill = {0: 0, 1: 0}
        cast_v = 0
        obev = 0
        osp_done = 0
        x_used = 0

        ldb = S("p4a_ldb")
        ldb.dinc(sync.dma_start(bo_row32.ap(), bo_d.ap()))
        nc.vector.wait_ge(ldb.h, ldb.v)
        dv.inc(nc.vector.tensor_copy(bo_row.ap(), bo_row32.ap()))

        ldw[0].dinc(sync.dma_start(wbuf4[0].ap(), wo_d.ap()[0:128, :]))
        for oc in range(NCH):
            sl = oc % 2
            if oc + 1 < NCH:
                if oc >= 1:
                    sync.wait_ge(dv.h, cast_v)
                ldw[(oc + 1) % 2].dinc(sync.dma_start(
                    wbuf4[(oc + 1) % 2].ap(), wo_d.ap()[(oc + 1) * 128:(oc + 2) * 128, :]))
            nc.vector.wait_ge(ldw[sl].h, ldw[sl].v)
            nc.vector.wait_ge(pe.h, pe.v)
            cast_v = dv.inc(nc.vector.tensor_copy(wbf4.ap(), wbuf4[sl].ap()))
            nc.tensor.wait_ge(dv.h, cast_v)
            for half in range(0, NCH, 8):
                nh = min(8, NCH - half)
                if half > 0 or oc > 0:
                    nc.tensor.wait_ge(dv.h, dv.v)
                for j in range(nh):
                    ins = nc.tensor.transpose(
                        ptr4[:, j * 128:(j + 1) * 128],
                        wbf4[:, (half + j) * 128:(half + j + 1) * 128], ident[:])
                    if j == nh - 1:
                        pe.inc(ins)
                nc.vector.wait_ge(pe.h, pe.v)
                dv.inc(nc.vector.tensor_copy(
                    woT_v[:, half:half + nh, oc * 128:(oc + 1) * 128],
                    ptr4[:, 0:nh * 128].rearrange("p (c n) -> p c n", n=128)))

        nc.tensor.wait_ge(dv.h, dv.v)
        ldo[0].dinc(sync.dma_start(
            o_sp[0].ap().rearrange("p (c n) -> p c n", n=512), so_v[:, :, 0:512]))
        ldx4[0].dinc(sync.dma_start(x_sp[0].ap(), x_d.ap()[0:128, :]))

        for sp in range(NSP):
            sl = sp % 2
            if sp + 1 < NSP:
                if sp >= 1:
                    sync.wait_ge(pe.h, osp_done)
                ldo[(sp + 1) % 2].dinc(sync.dma_start(
                    o_sp[(sp + 1) % 2].ap().rearrange("p (c n) -> p c n", n=512),
                    so_v[:, :, (sp + 1) * 512:(sp + 2) * 512]))

            nc.tensor.wait_ge(ldo[sl].h, ldo[sl].v)
            if obev:
                nc.tensor.wait_ge(ac.h, obev)
            for ic in range(NCH):
                pe.inc(nc.tensor.matmul(
                    pob[0:16, :], BoT[:, ic * 16:(ic + 1) * 16],
                    o_sp[sl][:, ic * 512:(ic + 1) * 512],
                    start=(ic == 0), stop=(ic == NCH - 1)))
            nc.scalar.wait_ge(pe.h, pe.v)
            obev = ac.inc(nc.scalar.activation(oBs4.ap(), pob[0:16, :], AF.Copy,
                                               scale=LORA_SCALING))

            for tt in range(4):
                t = sp * 4 + tt
                if t + 1 < NT:
                    if t >= 1:
                        sync.wait_ge(dv.h, x_used)
                    ldx4[(t + 1) % 2].dinc(sync.dma_start(
                        x_sp[(t + 1) % 2].ap(), x_d.ap()[(t + 1) * 128:(t + 2) * 128, :]))
                for hf in range(NHF):
                    psl = (t * NHF + hf) % 2
                    if resfree[psl]:
                        nc.tensor.wait_ge(dv.h, resfree[psl])
                    nc.tensor.wait_ge(ac.h, obev)
                    for ocs in range(2):
                        ocol = hf * 1024 + ocs * 512
                        for ic in range(NCH):
                            pe.inc(nc.tensor.matmul(
                                pa[psl][:, ocs * 512:(ocs + 1) * 512],
                                o_sp[sl][:, ic * 512 + tt * 128: ic * 512 + tt * 128 + 128],
                                woT[:, ic * D + ocol: ic * D + ocol + 512],
                                start=(ic == 0), stop=False, skip_group_check=True))
                        pe.inc(nc.tensor.matmul(
                            pa[psl][:, ocs * 512:(ocs + 1) * 512],
                            oBs4[0:16, tt * 128:(tt + 1) * 128],
                            ATo[0:16, ocol:ocol + 512],
                            start=False, stop=False, skip_group_check=True))
                        pe.inc(nc.tensor.matmul(
                            pa[psl][:, ocs * 512:(ocs + 1) * 512],
                            ones_bfr.ap(), bo_row[0:1, ocol:ocol + 512],
                            start=False, stop=True, skip_group_check=True))
                    nc.vector.wait_ge(pe.h, pe.v)
                    nc.vector.wait_ge(ldx4[t % 2].h, ldx4[t % 2].v)
                    if res_spill[psl]:
                        nc.vector.wait_ge(sto4[psl].h, res_spill[psl])
                    rv = dv.inc(nc.vector.tensor_tensor(
                        res4[psl].ap(), pa[psl][:],
                        x_sp[t % 2][:, hf * 1024:(hf + 1) * 1024], ALU.add))
                    resfree[psl] = rv
                    if hf == NHF - 1:
                        x_used = rv
                    nc.scalar.wait_ge(dv.h, rv)
                    sto4[psl].dinc(nc.scalar.dma_start(
                        out_d.ap()[t * 128:(t + 1) * 128, hf * 1024:(hf + 1) * 1024],
                        res4[psl].ap()))
                    res_spill[psl] = sto4[psl].v
            osp_done = pe.v

        sync.wait_ge(ldo[0].h, ldo[0].v)
        sync.wait_ge(ldo[1].h, ldo[1].v)
        sync.wait_ge(ldx4[0].h, ldx4[0].v)
        sync.wait_ge(ldx4[1].h, ldx4[1].v)
        sync.wait_ge(ldw[0].h, ldw[0].v)
        sync.wait_ge(ldw[1].h, ldw[1].v)
        nc.scalar.wait_ge(sto4[0].h, sto4[0].v)
        nc.scalar.wait_ge(sto4[1].h, sto4[1].v)

    PHASES.append(_phase4)
    es_p4a.close()

    es_p4b = ExitStack()
    swT = es_p4b.enter_context(nc.sbuf_tensor("swT", [128, NCH * D], BF16))
    fwb4 = [es_p4b.enter_context(nc.sbuf_tensor(f"fwb4{i}", [128, D], F32)) for i in range(2)]
    swsg = es_p4b.enter_context(nc.sbuf_tensor("swsg", [128, D], BF16))
    a_sp = [es_p4b.enter_context(nc.sbuf_tensor(f"a_sp{i}", [128, NCH * 512], BF16)) for i in range(2)]
    ord_b = [es_p4b.enter_context(nc.sbuf_tensor(f"ord{i}", [128, D], F32)) for i in range(2)]
    tf4 = [es_p4b.enter_context(nc.sbuf_tensor(f"tf4{i}", [128, 1024], F32)) for i in range(2)]
    res4b = [es_p4b.enter_context(nc.sbuf_tensor(f"res4b{i}", [128, 1024], F32)) for i in range(2)]
    al_bc = es_p4b.enter_context(nc.sbuf_tensor("al_bc", [128, D], F32))
    al_row = es_p4b.enter_context(nc.sbuf_tensor("al_row", [1, D], F32))
    fb_row = es_p4b.enter_context(nc.sbuf_tensor("fb_row", [1, D], BF16))
    fb_row32 = es_p4b.enter_context(nc.sbuf_tensor("fb_row32", [1, D], F32))
    rs4 = es_p4b.enter_context(nc.sbuf_tensor("rs4", [128, 4], F32))
    sw_scr = es_p4b.enter_context(nc.sbuf_tensor("sw_scr", [128, D], F32))
    pf = [es_p4b.enter_context(nc.psum_tensor(f"pf{i}", [128, 1024], F32)) for i in range(2)]
    ptr4b = es_p4b.enter_context(nc.psum_tensor("ptr4b", [128, 1024], BF16))
    pbc4 = es_p4b.enter_context(nc.psum_tensor("pbc4", [128, 512], F32))

    swT_v = swT.ap().rearrange("p (c n) -> p c n", n=D)
    sa_v = s_a.ap().rearrange("(c p) t -> p c t", p=128)

    def _phase5(sync):
        ldw = [S("p4b_ldw0"), S("p4b_ldw1")]
        lda = [S("p4b_lda0"), S("p4b_lda1")]
        ldr = [S("p4b_ldr0"), S("p4b_ldr1")]
        pe = S("p4b_pe")
        dv = S("p4b_dv")
        ac = S("p4b_ac")
        sto5 = [S("p4b_sto0"), S("p4b_sto1")]
        resfree = {0: 0, 1: 0}
        res_spill = {0: 0, 1: 0}
        sg_v = 0
        last_res_v = 0
        asp_done = 0
        o_used = 0

        ldb2 = S("p4b_ldb")
        ldb2.dinc(sync.dma_start(al_row.ap(), al_d.ap()))
        ldb2.dinc(sync.dma_start(fb_row32.ap(), fb_d.ap()))
        nc.vector.wait_ge(ldb2.h, ldb2.v)
        dv.inc(nc.vector.tensor_copy(fb_row.ap(), fb_row32.ap()))
        nc.tensor.wait_ge(ldb2.h, ldb2.v)
        for j in range(D // 512):
            if j > 0:
                nc.tensor.wait_ge(dv.h, dv.v)
            pe.inc(nc.tensor.matmul(pbc4[:], ones_f32r.ap(),
                                    al_row[0:1, j * 512:(j + 1) * 512],
                                    start=True, stop=True))
            nc.vector.wait_ge(pe.h, pe.v)
            dv.inc(nc.vector.tensor_copy(al_bc[:, j * 512:(j + 1) * 512], pbc4[:]))

        ldw[0].dinc(sync.dma_start(fwb4[0].ap(), fw_d.ap()[0:128, :]))
        for oc in range(NCH):
            sl = oc % 2
            if oc + 1 < NCH:
                if oc >= 1:
                    sync.wait_ge(ac.h, sg_v)
                    sync.wait_ge(dv.h, last_res_v)
                ldw[(oc + 1) % 2].dinc(sync.dma_start(
                    fwb4[(oc + 1) % 2].ap(), fw_d.ap()[(oc + 1) * 128:(oc + 2) * 128, :]))
            nc.vector.wait_ge(ldw[sl].h, ldw[sl].v)
            v0 = dv.inc(nc.vector.reduce_sum(rs4[:, 0:1], fwb4[sl].ap(), axis=AX.X))
            nc.vector.wait_ge(dv.h, v0)
            v1 = dv.inc(nc.vector.tensor_scalar_mul(rs4[:, 1:2], rs4[:, 0:1], INV_D))
            nc.vector.wait_ge(dv.h, v1)
            v2 = dv.inc(nc.vector.tensor_scalar(
                sw_scr.ap(), fwb4[sl].ap(), rs4[:, 1:2], None, ALU.subtract))
            nc.vector.wait_ge(dv.h, v2)
            v3 = dv.inc(nc.vector.reduce_sum(rs4[:, 2:3], sw_scr.ap(), axis=AX.X))
            nc.vector.wait_ge(dv.h, v3)
            v4 = dv.inc(nc.vector.tensor_scalar(
                rs4[:, 3:4], rs4[:, 2:3], NEG_INV_D, rs4[:, 1:2],
                ALU.mult, ALU.subtract))
            last_res_v = v2
            nc.scalar.wait_ge(dv.h, v4)
            nc.scalar.wait_ge(pe.h, pe.v)
            sg_v = ac.inc(nc.scalar.activation(swsg.ap(), fwb4[sl].ap(), AF.Sign,
                                               bias=rs4[:, 3:4]))
            nc.tensor.wait_ge(ac.h, sg_v)
            for half in range(0, NCH, 8):
                nh = min(8, NCH - half)
                if half > 0 or oc > 0:
                    nc.tensor.wait_ge(dv.h, dv.v)
                for j in range(nh):
                    ins = nc.tensor.transpose(
                        ptr4b[:, j * 128:(j + 1) * 128],
                        swsg[:, (half + j) * 128:(half + j + 1) * 128], ident[:])
                    if j == nh - 1:
                        pe.inc(ins)
                nc.vector.wait_ge(pe.h, pe.v)
                dv.inc(nc.vector.tensor_copy(
                    swT_v[:, half:half + nh, oc * 128:(oc + 1) * 128],
                    ptr4b[:, 0:nh * 128].rearrange("p (c n) -> p c n", n=128)))

        nc.tensor.wait_ge(dv.h, dv.v)
        lda[0].dinc(sync.dma_start(
            a_sp[0].ap().rearrange("p (c n) -> p c n", n=512), sa_v[:, :, 0:512]))
        ldr[0].dinc(sync.dma_start(ord_b[0].ap(), out_d.ap()[0:128, :]))

        NSP2 = T // 512
        for sp in range(NSP2):
            sl = sp % 2
            if sp + 1 < NSP2:
                if sp >= 1:
                    sync.wait_ge(pe.h, asp_done)
                lda[(sp + 1) % 2].dinc(sync.dma_start(
                    a_sp[(sp + 1) % 2].ap().rearrange("p (c n) -> p c n", n=512),
                    sa_v[:, :, (sp + 1) * 512:(sp + 2) * 512]))
            nc.tensor.wait_ge(lda[sl].h, lda[sl].v)
            for tt in range(4):
                t = sp * 4 + tt
                if t + 1 < NT:
                    if t >= 1:
                        sync.wait_ge(dv.h, o_used)
                    ldr[(t + 1) % 2].dinc(sync.dma_start(
                        ord_b[(t + 1) % 2].ap(), out_d.ap()[(t + 1) * 128:(t + 2) * 128, :]))
                for hf in range(NHF):
                    psl = (t * NHF + hf) % 2
                    if resfree[psl]:
                        nc.tensor.wait_ge(dv.h, resfree[psl])
                    for ocs in range(2):
                        ocol = hf * 1024 + ocs * 512
                        for ic in range(NCH):
                            pe.inc(nc.tensor.matmul(
                                pf[psl][:, ocs * 512:(ocs + 1) * 512],
                                a_sp[sl][:, ic * 512 + tt * 128: ic * 512 + tt * 128 + 128],
                                swT[:, ic * D + ocol: ic * D + ocol + 512],
                                start=(ic == 0), stop=False, skip_group_check=True))
                        pe.inc(nc.tensor.matmul(
                            pf[psl][:, ocs * 512:(ocs + 1) * 512],
                            ones_bfr.ap(), fb_row[0:1, ocol:ocol + 512],
                            start=False, stop=True, skip_group_check=True))
                    nc.vector.wait_ge(pe.h, pe.v)
                    nc.vector.wait_ge(ldr[t % 2].h, ldr[t % 2].v)
                    v_tf = dv.inc(nc.vector.tensor_tensor(
                        tf4[psl].ap(), pf[psl][:],
                        al_bc[:, hf * 1024:(hf + 1) * 1024], ALU.mult))
                    resfree[psl] = v_tf
                    nc.vector.wait_ge(dv.h, v_tf)
                    if res_spill[psl]:
                        nc.vector.wait_ge(sto5[psl].h, res_spill[psl])
                    rv2 = dv.inc(nc.vector.tensor_tensor(
                        res4b[psl].ap(), tf4[psl].ap(),
                        ord_b[t % 2][:, hf * 1024:(hf + 1) * 1024], ALU.add))
                    if hf == NHF - 1:
                        o_used = rv2
                    nc.scalar.wait_ge(dv.h, rv2)
                    sto5[psl].dinc(nc.scalar.dma_start(
                        out_d.ap()[t * 128:(t + 1) * 128, hf * 1024:(hf + 1) * 1024],
                        res4b[psl].ap()))
                    res_spill[psl] = sto5[psl].v
            asp_done = pe.v

        sync.wait_ge(lda[0].h, lda[0].v)
        sync.wait_ge(lda[1].h, lda[1].v)
        sync.wait_ge(ldr[0].h, ldr[0].v)
        sync.wait_ge(ldr[1].h, ldr[1].v)
        nc.scalar.wait_ge(sto5[0].h, sto5[0].v)
        nc.scalar.wait_ge(sto5[1].h, sto5[1].v)
        sync.wait_ge(sto5[0].h, sto5[0].v)
        sync.wait_ge(sto5[1].h, sto5[1].v)

    PHASES.append(_phase5)
    es_p4b.close()

    def _join():
        engines = [nc.tensor, nc.vector, nc.scalar, nc.gpsimd, nc.sync]
        for eng in engines:
            for sm in sems.values():
                if sm.v:
                    eng.wait_ge(sm.h, sm.v)

    with nc.Block() as block:
        @block.sync
        def _(sync):
            for pi, f in enumerate(PHASES):
                f(sync)
                if pi + 1 < len(PHASES):
                    _join()

    nc.finalize()
    return nc


def make_in_map(b, x, inputs):
    def row(v):
        return np.asarray(v, dtype=np.float32).reshape(1, -1)
    return {
        "x": x[b],
        "wq": inputs["L0_attn_Wq"], "wk": inputs["L0_attn_Wk"],
        "wv": inputs["L0_attn_Wv"], "wo": inputs["L0_attn_Wo"],
        "bq": inputs["L0_attn_bq"], "bk": inputs["L0_attn_bk"],
        "bv": inputs["L0_attn_bv"], "bo": row(inputs["L0_attn_bo"]),
        "Blq": inputs["L0_attn_Bq"], "Blk": inputs["L0_attn_Bk"],
        "Blv": inputs["L0_attn_Bv"], "Blo": inputs["L0_attn_Bo"],
        "Alq": inputs["L0_attn_Aq"], "Alk": inputs["L0_attn_Ak"],
        "Alv": inputs["L0_attn_Av"], "Alo": inputs["L0_attn_Ao"],
        "gm": row(inputs["L0_ffn_gamma"]), "bt": row(inputs["L0_ffn_beta"]),
        "fw": inputs["L0_ffn_weight"], "fb": row(inputs["L0_ffn_bias"]),
        "al": row(inputs["L0_ffn_alpha"]),
    }


def kernel(**inputs):
    x = np.asarray(inputs["L0_attn_x"], dtype=np.float32)
    B, T, D = x.shape
    nc = build_nc(T, D)
    in_maps = [make_in_map(b, x, inputs) for b in range(B)]
    res = run_bass_kernel_spmd(nc, in_maps, core_ids=list(range(B)))
    out = np.stack([np.asarray(res.results[b]["out"]) for b in range(B)], axis=0)
    return out.astype(np.float32)
